# version 04
# baseline (speedup 1.0000x reference)
import os
import numpy as np
import ml_dtypes
import concourse.bass as bass
import concourse.mybir as mybir
from concourse.bass_utils import run_bass_kernel_spmd

F32 = mybir.dt.float32
BF16 = mybir.dt.bfloat16
I32 = mybir.dt.int32
AF = mybir.ActivationFunctionType
ALU = mybir.AluOpType
AX = mybir.AxisListType

NCORES = 8
D = 2048
NTOK = 16384
ROWS = 2048
NT_OWN = 16
KROWS = 8192
NT_K = 64
H = 16
HKV = 4
FD = 768
FC = 6
I_FFN = 8192
SEQ = 4096
LAMBDA_INIT = 0.8 - 0.6 * float(np.exp(-0.3 * 12))
QSCALE = float(128 ** 0.5)
DELTA_W = 129 * H * FC

BUILD_PHASES = ("attn_a",)
DEBUG_TAPS = ()

_CACHE = {}


class Sem:
    def __init__(self, h):
        self.h = h
        self.n = 0


class _Tok:
    def __init__(self):
        self.chain = []

    def then_inc(self, sem, k=1):
        sem.n += k
        self.chain.append((sem, k))
        return self


class Rec:
    def __init__(self, name):
        self.name = name
        self.ops = []

    def __getattr__(self, attr):
        def rec(*a, **kw):
            tok = _Tok()
            self.ops.append((attr, a, kw, tok))
            return tok
        return rec

    def raw(self, fn):
        self.ops.append((None, (fn,), {}, None))

    def wait(self, sem, val=None):
        v = sem.n if val is None else val
        self.ops.append(("wait_ge", (sem.h, v), {}, None))

    def bump(self, sem, k=1):
        for i in range(len(self.ops) - 1, -1, -1):
            attr, a, kw, tok = self.ops[i]
            if attr not in (None, "wait_ge") and tok is not None:
                assert not tok.chain, (
                    "bump would stack a second update on one instruction; "
                    "restructure the sync at this site")
                tok.then_inc(sem, k)
                return
        t = _Tok()
        t.then_inc(sem, k)
        self.ops.append(("nop", (), {}, t))

    DRAIN_AFTER = {
        "tensor_tensor", "tensor_scalar", "tensor_single_scalar",
        "tensor_scalar_add", "tensor_scalar_sub", "tensor_scalar_mul",
        "tensor_scalar_max", "tensor_scalar_min", "tensor_copy",
        "tensor_relu", "tensor_reduce", "reduce_sum", "reduce_max",
        "scalar_tensor_tensor", "reciprocal", "memset", "activation",
        "copy", "square", "mul", "add", "sign", "sqrt",
    }

    def replay(self, eng):
        autodrain = self.name in ("vector", "scalar", "gpsimd")
        n = len(self.ops)
        for i, (attr, a, kw, tok) in enumerate(self.ops):
            if attr is None:
                a[0](eng)
                continue
            inst = getattr(eng, attr)(*a, **kw)
            if tok is not None:
                for (sem, k) in tok.chain:
                    inst = inst.then_inc(sem.h, k)
            if (autodrain and attr in self.DRAIN_AFTER
                    and not (i + 1 < n and self.ops[i + 1][0] == "drain")):
                eng.drain()


class Arena:
    def __init__(self, nc, base, top):
        self.nc = nc
        self.base = base
        self.top = top
        self.off = base
        self.marks = []

    def alloc(self, name, shape, dtype, align=64):
        per_part = int(np.prod(shape[1:])) * mybir.dt.size(dtype)
        self.off = (self.off + align - 1) // align * align
        t = self.nc.alloc_sbuf_tensor_at(name, list(shape), dtype,
                                         offset=self.off)
        self.off += per_part
        assert self.off <= self.top, (
            f"SBUF overflow at {name}: {self.off} > {self.top}")
        return t

    def mark(self):
        self.marks.append(self.off)

    def release(self):
        self.off = self.marks.pop()


def build_program():
    detect = os.environ.get("KRACE", "1") != "0"
    nc = bass.Bass("TRN2", target_bir_lowering=False,
                   detect_race_conditions=detect)

    def din(name, shape, dtype=F32):
        return nc.dram_tensor(name, list(shape), dtype, kind="ExternalInput").ap()

    xo = din("xo", [ROWS, D])
    xk = din("xk", [KROWS, D])
    wkvmbT = din("wkvmbT", [D, 3072])
    wqgT = din("wqgT", [D, 4096])
    bias_kvmb = din("bias_kvmb", [1, 3072])
    bias_qg = din("bias_qg", [1, 4096])
    mem_in = din("mem", [H * FD, 128])
    normT_in = din("normT", [FD, H])
    wqkvbT = din("wqkvbT", [D, 6144])
    woT = din("woT", [D, D])
    lam_in = din("lam", [128, 256])
    sublnw_r = din("sublnw_r", [128, 128])
    cosq = din("cosq", [ROWS, 32])
    sinq = din("sinq", [ROWS, 32])
    cosk = din("cosk", [ROWS, 32])
    sink = din("sink", [ROWS, 32])
    wg_f = din("wg_f", [D, I_FFN])
    wu_f = din("wu_f", [D, I_FFN])
    wd_f = din("wd_f", [I_FFN, D])
    ident_in = din("ident", [128, 128], BF16)
    ones_in = din("onesrow", [1, 128], BF16)
    meta_in = din("meta", [1, 8], I32)

    y = nc.dram_tensor("y", [ROWS, D], F32, kind="ExternalOutput").ap()
    taps = {}
    for t in DEBUG_TAPS:
        taps[t] = nc.dram_tensor("tap_" + t, [ROWS, D], F32,
                                 kind="ExternalOutput").ap()

    xbuf = nc.dram_tensor("xbuf", [ROWS, D], F32)
    wkvmb_c = nc.dram_tensor("wkvmb_c", [D, 3072], BF16)
    wqg_c = nc.dram_tensor("wqg_c", [D, 4096], BF16)
    wqkvb_c = nc.dram_tensor("wqkvb_c", [D, 6144], BF16)
    wo_c = nc.dram_tensor("wo_c", [D, D], BF16)
    wgf_c = nc.dram_tensor("wgf_c", [D, I_FFN], BF16)
    wuf_c = nc.dram_tensor("wuf_c", [D, I_FFN], BF16)
    wdf_c = nc.dram_tensor("wdf_c", [I_FFN, D], BF16)

    sh_delta = nc.dram_tensor("sh_delta", [2, 128, DELTA_W], F32,
                              addr_space="Shared")
    sh_qkv = nc.dram_tensor("sh_qkv", [2, 3, 1024, ROWS], BF16,
                            addr_space="Shared")
    sh_attn = nc.dram_tensor("sh_attn", [2, 8, 128, ROWS], BF16,
                             addr_space="Shared")
    qT_d = nc.dram_tensor("qT_d", [1024, ROWS], BF16)
    kT_d = nc.dram_tensor("kT_d", [1024, ROWS], BF16)
    v_d = nc.dram_tensor("v_d", [ROWS, 1024], BF16)
    at_d = nc.dram_tensor("at_d", [8, 128, ROWS], BF16)
    md_d = nc.dram_tensor("md_d", [16, SEQ], F32)
    sh_flag = nc.dram_tensor("sh_flag", [2, 16], I32, addr_space="Shared")

    PS = [nc.alloc_psum_tensor(f"ps{i}", [128, 512], F32) for i in range(8)]
    PSb = [p[:].bitcast(BF16) for p in PS]

    SP, PL, PE, VE, AC = (Rec("sync"), Rec("gpsimd"), Rec("tensor"),
                          Rec("vector"), Rec("scalar"))
    sem_guards = []

    def mksem(name):
        g = nc.semaphore(name)
        h = g.__enter__()
        sem_guards.append(g)
        return Sem(h)

    d_x = mksem("d_x")
    d_xp = [mksem("d_x0"), mksem("d_x1")]
    d_w = mksem("d_w")
    d_wp = [mksem("d_w0"), mksem("d_w1")]
    d_st = mksem("d_st")
    d_pl = mksem("d_pl")
    d_acc = mksem("d_acc")
    d_out = mksem("d_out")
    s_pe = mksem("s_pe")
    s_v = mksem("s_v")
    s_a = mksem("s_a")
    s_pl = mksem("s_pl")
    pd = mksem("pd")
    phz = mksem("phz")

    ALL = (SP, PL, PE, VE, AC)

    CKPT = {}

    def ckpt(name):
        CKPT[name] = {id(r): len(r.ops) for r in ALL}

    def phase_barrier():
        for dsem in (d_x, d_xp[0], d_xp[1], d_w, d_wp[0], d_wp[1], d_st,
                     d_out):
            SP.wait(dsem, dsem.n)
        SP.nop().then_inc(phz, 1)
        for dsem in (d_pl, d_acc):
            PL.wait(dsem, dsem.n)
        PL.nop().then_inc(phz, 1)
        for r in (PE, VE, AC):
            r.drain()
            r.bump(phz, 1)
        tgt = phz.n
        for r in ALL:
            r.wait(phz, tgt)

    bar = {"k": 0}

    def pair_barrier(extra=None):
        bar["k"] += 1
        K = bar["k"]
        if os.environ.get("KSOLO") == "1":
            return
        psem = mksem(f"pb{K}")

        def body(g):
            rpar = g.alloc_register(f"b{K}_rpar")
            rflag = g.alloc_register(f"b{K}_rflag")
            rcond = g.alloc_register(f"b{K}_rcond")
            rpd = g.alloc_register(f"b{K}_rpd")
            g.reg_load(rpar, meta[0:1, 0:1])
            g.reg_mov(rpd, 0)

            def dma(dst, src):
                g.reg_add(rpd, rpd, 16)
                g.dma_start(out=dst, in_=src).then_inc(psem.h, 16)
                g.wait_ge(psem.h, rpd)

            g.memset(flagk[:], K)
            g.drain()
            with g.If_eq(rpar, 0):
                dma(sh_flag[0, :][None, :], flagk[:])
            with g.Else():
                dma(sh_flag[1, :][None, :], flagk[:])
            g.reg_mov(rflag, 0)

            def cond():
                g.reg_alu(rcond, rflag, K, ALU.is_lt)
                return rcond
            with g.While(cond):
                with g.If_eq(rpar, 0):
                    dma(flagp[:], sh_flag[1, :][None, :])
                with g.Else():
                    dma(flagp[:], sh_flag[0, :][None, :])
                g.reg_load(rflag, flagp[0:1, 0:1])
            if extra is not None:
                extra(g, dma)
            for r in (rpar, rflag, rcond, rpd):
                g.free_register(r)
        PL.raw(body)

    def emit_rsqrt(rec, out_ap, in_ap, scr_a, scr_y, floor=1e-30, n_iter=3):
        rec.tensor_scalar_max(scr_a, in_ap, float(floor))
        rec.drain()
        bi = scr_y.bitcast(I32)
        rec.tensor_scalar(out=bi, in0=scr_a.bitcast(I32), scalar1=1,
                          scalar2=None, op0=ALU.arith_shift_right)
        rec.drain()
        rec.tensor_scalar(out=bi, in0=bi, scalar1=-1, scalar2=0x5f3759df,
                          op0=ALU.mult, op1=ALU.add)
        rec.drain()
        for _ in range(n_iter):
            rec.tensor_tensor(out=out_ap, in0=scr_y, in1=scr_y, op=ALU.mult)
            rec.drain()
            rec.tensor_tensor(out=out_ap, in0=out_ap, in1=scr_a, op=ALU.mult)
            rec.drain()
            rec.tensor_scalar(out=out_ap, in0=out_ap, scalar1=-0.5,
                              scalar2=1.5, op0=ALU.mult, op1=ALU.add)
            rec.drain()
            rec.tensor_tensor(out=scr_y, in0=scr_y, in1=out_ap, op=ALU.mult)
            rec.drain()
        rec.tensor_copy(out=out_ap, in_=scr_y)
        rec.drain()

    ar = Arena(nc, nc.sbuf_base, nc.sbuf_top)
    ident = ar.alloc("identsb", [128, 128], BF16)
    onesr = ar.alloc("onesr", [1, 128], BF16)
    meta = ar.alloc("metasb", [1, 8], I32)
    flagk = ar.alloc("flagk", [1, 16], I32)
    flagp = ar.alloc("flagp", [1, 16], I32)
    stats16 = ar.alloc("stats16", [128, NT_OWN, 2], F32)
    SCR = ar.alloc("SCR", [128, D], F32)
    ar.mark()
    mem_aug = ar.alloc("mem_aug", [128, H * FC, 128], BF16)
    nrmcol = ar.alloc("nrmcol", [128, FC, H], BF16)
    nrm32 = ar.alloc("nrm32", [128, FC, H], F32)
    brow_kvmb = ar.alloc("brow_kvmb", [1, 3072], BF16)
    brow_qg = ar.alloc("brow_qg", [1, 4096], BF16)

    SP.dma_start(out=ident[:], in_=ident_in).then_inc(d_x, 16)
    SP.dma_start(out=onesr[:], in_=ones_in).then_inc(d_x, 16)
    SP.dma_start(out=meta[:], in_=meta_in).then_inc(d_x, 16)
    CONSTS_T = d_x.n

    PLregs = {}

    def _init_regs(g):
        g.wait_ge(d_x.h, CONSTS_T)
        PLregs["rpar"] = g.alloc_register("rpar_glob")
        g.reg_load(PLregs["rpar"], meta[0:1, 0:1])
    PL.raw(_init_regs)

    mem_r = mem_in.rearrange("(g p) d -> p g d", p=128)
    normT_r = normT_in.rearrange("(fc p) h -> p fc h", p=128)
    PL.dma_start(out=mem_aug[:], in_=mem_r).then_inc(d_pl, 16)
    PL.wait(d_pl, d_pl.n)
    PL.dma_start(out=nrm32[:], in_=normT_r).then_inc(d_pl, 16)
    PL.wait(d_pl, d_pl.n)
    PL.dma_start(out=nrmcol[:], in_=normT_r).then_inc(d_pl, 16)
    PL.wait(d_pl, d_pl.n)
    PL.dma_start(out=brow_kvmb[:], in_=bias_kvmb).then_inc(d_pl, 16)
    PL.wait(d_pl, d_pl.n)
    PL.dma_start(out=brow_qg[:], in_=bias_qg).then_inc(d_pl, 16)
    PL.wait(d_pl, d_pl.n)
    P0_POOL_T = d_pl.n

    CACHE_T = {}
    for dst, src in ((wqkvb_c, wqkvbT), (wo_c, woT)):
        PL.dma_start(out=dst.ap(), in_=src).then_inc(d_pl, 16)
        PL.wait(d_pl, d_pl.n)
        CACHE_T[dst.name] = [(d_pl, d_pl.n)]
    ar.mark()
    sf32 = [ar.alloc(f"cst32_{i}", [128, 8192], F32) for i in range(2)]
    sbf = [ar.alloc(f"cstbf_{i}", [128, 8192], BF16) for i in range(2)]
    a_after = [None, None]
    cb = 0
    for dst, src, R, C in ((wkvmb_c, wkvmbT, D, 3072),
                           (wqg_c, wqgT, D, 4096),
                           (wgf_c, wg_f, D, I_FFN),
                           (wuf_c, wu_f, D, I_FFN),
                           (wdf_c, wd_f, I_FFN, D)):
        src_r = src.rearrange("(n p) c -> n p c", p=128)
        dst_r = dst.ap().rearrange("(n p) c -> n p c", p=128)
        for n in range(R // 128):
            pi = cb % 2
            if a_after[pi] is not None:
                SP.wait(s_a, a_after[pi])
            SP.dma_start(out=sf32[pi][:, 0:C], in_=src_r[n]
                         ).then_inc(d_xp[pi], 16)
            AC.wait(d_xp[pi], d_xp[pi].n)
            AC.wait(d_wp[pi], d_wp[pi].n)
            AC.copy(out=sbf[pi][:, 0:C], in_=sf32[pi][:, 0:C])
            AC.bump(s_a, 1)
            a_after[pi] = s_a.n
            AC.dma_start(out=dst_r[n], in_=sbf[pi][:, 0:C]
                         ).then_inc(d_wp[pi], 16)
            cb += 1
        CACHE_T[dst.name] = [(d_wp[0], d_wp[0].n), (d_wp[1], d_wp[1].n)]
    for r in (SP, VE, PL):
        r.wait(d_wp[0], d_wp[0].n)
        r.wait(d_wp[1], d_wp[1].n)
    ar.release()
    ckpt("p0")

    ar.mark()
    kstat_m = ar.alloc("kstat_m", [128, NT_K], F32)
    kstat_r = ar.alloc("kstat_r", [128, NT_K], F32)
    kscr1 = ar.alloc("kscr1", [128, NT_K], F32)
    kscr2 = ar.alloc("kscr2", [128, NT_K], F32)
    xt2 = [ar.alloc(f"xt{i}", [128, D], F32) for i in range(2)]
    hsb = ar.alloc("hsb", [128, D], BF16)
    hsT_g = [ar.alloc(f"hsTg{i}", [128, 16, 128], BF16) for i in range(4)]
    wbuf = [ar.alloc(f"wbuf{i}", [128, 16, 256], BF16) for i in range(2)]
    kst = [ar.alloc(f"kst{i}", [128, 512], F32) for i in range(4)]
    vst = [ar.alloc(f"vst{i}", [128, 512], F32) for i in range(4)]
    mbst = [ar.alloc(f"mbst{i}", [128, D], BF16) for i in range(4)]
    xr = ar.alloc("xr", [128, HKV, 256], F32)
    dpk32 = ar.alloc("dpk32", [128, HKV, FD], F32)
    dpkb = [ar.alloc(f"dpkb{i}", [128, HKV * FD], BF16) for i in range(4)]
    dpkT = ar.alloc("dpkT", [128, HKV * FC, 128], BF16)
    wmv = [ar.alloc(f"wmv{i}", [128, H, 129], BF16) for i in range(4)]
    n2 = ar.alloc("n2", [128, HKV], F32)
    rsi = ar.alloc("rsi", [128, HKV], F32)
    sca = ar.alloc("sca", [128, HKV], F32)
    scb = ar.alloc("scb", [128, HKV], F32)
    s1e8 = ar.alloc("s1e8", [128, HKV], F32)
    dDp = ar.alloc("dDp", [128, H], F32)
    recD = ar.alloc("recD", [128, H], F32)
    negD = ar.alloc("negD", [128, H], F32)
    dcoef = ar.alloc("dcoef", [128, H], F32)
    t128 = ar.alloc("t128", [128, 128], F32)
    stg = ar.alloc("stg", [128, 8, 387], F32)

    xk_t = xk.rearrange("(n p) d -> n p d", p=128)

    T_x = []
    SP.dma_start(out=xt2[0][:], in_=xk_t[0]).then_inc(d_xp[0], 16)
    T_x.append((0, d_xp[0].n))
    for T in range(NT_K):
        if T + 1 < NT_K:
            if T >= 1:
                SP.wait(s_v, s_v.n)
            SP.dma_start(out=xt2[(T + 1) % 2][:], in_=xk_t[T + 1]
                         ).then_inc(d_xp[(T + 1) % 2], 16)
            T_x.append(((T + 1) % 2, d_xp[(T + 1) % 2].n))
        xt = xt2[T % 2][:]
        VE.wait(d_xp[T_x[T][0]], T_x[T][1])
        VE.reduce_sum(kstat_m[:, T:T + 1], xt, axis=AX.X)
        VE.drain()
        VE.tensor_scalar(out=kstat_m[:, T:T + 1], in0=kstat_m[:, T:T + 1],
                         scalar1=1.0 / D, scalar2=None, op0=ALU.mult)
        VE.drain()
        VE.tensor_scalar(out=SCR[:], in0=xt, scalar1=kstat_m[:, T:T + 1],
                         scalar2=None, op0=ALU.subtract)
        VE.drain()
        VE.tensor_tensor(out=SCR[:], in0=SCR[:], in1=SCR[:], op=ALU.mult)
        VE.drain()
        VE.reduce_sum(kstat_r[:, T:T + 1], SCR[:], axis=AX.X)
        VE.bump(s_v, 1)
    VE.drain()
    VE.tensor_scalar(out=kstat_r[:], in0=kstat_r[:], scalar1=1.0 / D,
                     scalar2=1e-5, op0=ALU.mult, op1=ALU.add)
    VE.drain()
    emit_rsqrt(VE, kstat_r[:], kstat_r[:], kscr1[:], kscr2[:])
    VE.bump(s_v, 1)
    ckpt("p1a")

    PE.wait(d_x, CONSTS_T)
    PE.wait(d_pl, P0_POOL_T)
    for _sm, _vv in CACHE_T["wkvmb_c"]:
        PE.wait(_sm, _vv)

    v_sv_hsT = {}
    for G in range(16):
        VE.wait(s_a, s_a.n)
        for t in range(4):
            T = 4 * G + t
            xt = xt2[T % 2][:]
            SP.wait(s_v, s_v.n)
            SP.dma_start(out=xt2[T % 2][:], in_=xk_t[T]
                         ).then_inc(d_xp[T % 2], 16)
            VE.wait(d_xp[T % 2], d_xp[T % 2].n)
            VE.tensor_scalar(out=SCR[:], in0=xt,
                             scalar1=kstat_m[:, T:T + 1],
                             scalar2=kstat_r[:, T:T + 1],
                             op0=ALU.subtract, op1=ALU.mult)
            VE.tensor_copy(out=hsb[:], in_=SCR[:])
            VE.bump(s_v, 1)
            if G == 0 and t == 0:
                ckpt("m_ln")
            PE.wait(s_v, s_v.n)
            for half in range(2):
                for j in range(8):
                    c = half * 8 + j
                    PE.transpose(PSb[6][:, 128 * j:128 * (j + 1)],
                                 hsb[:, 128 * c:128 * (c + 1)], ident[:])
                PE.bump(s_pe, 1)
                if G == 0 and t == 0 and half == 0:
                    ckpt("m_tr")
                AC.wait(s_pe, s_pe.n)
                AC.copy(out=hsT_g[t][:, 8 * half:8 * half + 8, :]
                        .rearrange("p a b -> p (a b)"), in_=PSb[6][:, 0:1024])
                AC.bump(s_a, 1)
                if G == 0 and t == 0 and half == 0:
                    ckpt("m_cp")
                if G == 0 and t == 0 and half == 1:
                    ckpt("m_cp2")
                PE.wait(s_a, s_a.n)
            VE.wait(s_a, s_a.n)
            if G == 0:
                ckpt(f"p1fa{t}")

        for ob in range(12):
            buf = wbuf[ob % 2]
            if G > 0 or ob > 1:
                SP.wait(s_pe, s_pe.n)
            SP.dma_start(
                out=buf[:],
                in_=wkvmb_c.ap()[:, 256 * ob:256 * (ob + 1)]
                .rearrange("(dc p) o -> p dc o", p=128)
                ).then_inc(d_wp[ob % 2], 16)
            PE.wait(d_wp[ob % 2], d_wp[ob % 2].n)
            for t in range(4):
                psu = PS[2 + t][:, 256 * (ob % 2):256 * (ob % 2 + 1)]
                for dc in range(16):
                    PE.matmul(psu, hsT_g[t][:, dc, :], buf[:, dc, :],
                              start=(dc == 0), stop=False)
                PE.matmul(psu, onesr[:],
                          brow_kvmb[:, 256 * ob:256 * (ob + 1)],
                          start=False, stop=True)
            PE.bump(s_pe, 1)
            if ob % 2 == 1:
                for t in range(4):
                    seg = ob // 2
                    if seg == 0:
                        VE.wait(s_pe, s_pe.n)
                        VE.tensor_copy(out=kst[t][:], in_=PS[2 + t][:])
                        VE.bump(s_v, 1)
                    elif seg == 1:
                        VE.wait(s_pe, s_pe.n)
                        VE.tensor_copy(out=vst[t][:], in_=PS[2 + t][:])
                        VE.bump(s_v, 1)
                    else:
                        AC.wait(s_pe, s_pe.n)
                        AC.activation(out=mbst[t][:, 512 * (seg - 2):
                                                  512 * (seg - 1)],
                                      in_=PS[2 + t][:], func=AF.Sigmoid)
                        AC.bump(s_a, 1)
                PE.wait(s_v, s_v.n)
                PE.wait(s_a, s_a.n)
            if G == 0:
                ckpt(f"p1ob{ob}")

        for t in range(4):
            T = 4 * G + t
            kv = kst[t][:].rearrange("p (h d) -> p h d", h=HKV)
            VE.tensor_relu(xr[:, :, 0:128], kv)
            VE.tensor_scalar(out=xr[:, :, 128:256], in0=kv, scalar1=-1.0,
                             scalar2=0.0, op0=ALU.mult, op1=ALU.max)
            VE.drain()
            for j in (1, 2, 3):
                o = 256 * (j - 1)
                VE.tensor_tensor(out=dpk32[:, :, o + j:o + 256],
                                 in0=xr[:, :, j:256],
                                 in1=xr[:, :, 0:256 - j], op=ALU.mult)
                VE.tensor_tensor(out=dpk32[:, :, o:o + j],
                                 in0=xr[:, :, 0:j],
                                 in1=xr[:, :, 256 - j:256], op=ALU.mult)
            VE.drain()
            VE.tensor_copy(out=dpkb[t][:], in_=dpk32[:]
                           .rearrange("p a b -> p (a b)"))
            VE.bump(s_v, 1)
            AC.wait(s_v, s_v.n)
            for hh in range(HKV):
                AC.activation(out=SCR[:, 0:FD], in_=dpk32[:, hh, :],
                              func=AF.Square,
                              accum_out=n2[:, hh:hh + 1])
            AC.bump(s_a, 1)
            VE.wait(s_a, s_a.n)
            emit_rsqrt(VE, rsi[:], n2[:], sca[:], scb[:], floor=1e-24)
            VE.tensor_tensor(out=s1e8[:], in0=n2[:], in1=rsi[:], op=ALU.mult)
            VE.drain()
            VE.tensor_scalar(out=s1e8[:], in0=s1e8[:], scalar1=1e-8,
                             scalar2=None, op0=ALU.mult)
            VE.drain()

            PE.wait(s_v, s_v.n)
            for rnd in range(3):
                for j in range(8):
                    c = rnd * 8 + j
                    PE.transpose(PSb[6][:, 128 * j:128 * (j + 1)],
                                 dpkb[t][:, 128 * c:128 * (c + 1)], ident[:])
                PE.bump(s_pe, 1)
                AC.wait(s_pe, s_pe.n)
                AC.copy(out=dpkT[:, 8 * rnd:8 * rnd + 8, :]
                        .rearrange("p a b -> p (a b)"), in_=PSb[6][:, 0:1024])
                AC.bump(s_a, 1)
                PE.wait(s_a, s_a.n)

            if t > 0 or G > 0:
                PE.wait(s_v, s_v.n)
            for g in range(HKV):
                pnum = PS[g]
                for fc in range(FC):
                    lt = dpkT[:, g * FC + fc, :]
                    rhs = mem_aug[:, (4 * g) * FC + fc:
                                  (4 * g + 3) * FC + fc + 1:FC, :]
                    PE.matmul(pnum[:], lt, rhs,
                              start=(fc == 0), stop=(fc == FC - 1))
                    PE.matmul(PS[7][:, 4 * g:4 * g + 4], lt,
                              nrmcol[:, fc, 4 * g:4 * g + 4],
                              start=(fc == 0), stop=(fc == FC - 1))
            PE.bump(s_pe, 1)

            VE.wait(s_pe, s_pe.n)
            VE.wait(s_a, s_a.n)
            for g in range(HKV):
                VE.tensor_scalar(out=dDp[:, 4 * g:4 * g + 4],
                                 in0=PS[7][:, 4 * g:4 * g + 4],
                                 scalar1=s1e8[:, g:g + 1], scalar2=None,
                                 op0=ALU.add)
            VE.drain()
            VE.reciprocal(recD[:], dDp[:])
            VE.drain()
            VE.tensor_scalar(out=negD[:], in0=recD[:],
                             scalar1=-1.0, scalar2=None, op0=ALU.mult)
            for g in range(HKV):
                VE.tensor_scalar(out=dcoef[:, 4 * g:4 * g + 4],
                                 in0=dDp[:, 4 * g:4 * g + 4],
                                 scalar1=rsi[:, g:g + 1], scalar2=-1.0,
                                 op0=ALU.mult, op1=ALU.mult)
            VE.drain()
            VE.tensor_scalar(out=dcoef[:], in0=dcoef[:],
                             scalar1=1.0, scalar2=0.0,
                             op0=ALU.add, op1=ALU.max)
            VE.drain()
            VE.tensor_scalar(out=dcoef[:], in0=dcoef[:],
                             scalar1=1.0, scalar2=None, op0=ALU.min)
            VE.drain()
            for g in range(HKV):
                VE.tensor_scalar(out=dcoef[:, 4 * g:4 * g + 4],
                                 in0=dcoef[:, 4 * g:4 * g + 4],
                                 scalar1=rsi[:, g:g + 1], scalar2=None,
                                 op0=ALU.mult)
            VE.drain()
            for g in range(HKV):
                for j in range(4):
                    h = 4 * g + j
                    VE.scalar_tensor_tensor(
                        out=t128[:], in0=PS[g][:, 128 * j:128 * (j + 1)],
                        scalar=negD[:, h:h + 1],
                        in1=vst[t][:, 128 * g:128 * (g + 1)],
                        op0=ALU.mult, op1=ALU.add)
                    VE.drain()
                    VE.scalar_tensor_tensor(
                        out=wmv[t][:, h, 0:128], in0=t128[:],
                        scalar=rsi[:, g:g + 1],
                        in1=mbst[t][:, 128 * h:128 * (h + 1)],
                        op0=ALU.mult, op1=ALU.mult)
                    VE.drain()
            VE.bump(s_v, 1)
            VE.tensor_copy(out=wmv[t][:, :, 128:129],
                           in_=dcoef[:].rearrange("p (h o) -> p h o", o=1))
            VE.drain()
            VE.bump(s_v, 1)
            v_sv_hsT[("wmv", G, t)] = s_v.n
            if G == 0:
                ckpt(f"p1fb{t}")

        PE.wait(s_v, v_sv_hsT[("wmv", G, 3)])
        for rnd in range(4):
            for sl in range(24):
                j = 24 * rnd + sl
                h, fc = j // FC, j % FC
                g = h // 4
                bank, slot = sl // 3, sl % 3
                out = PS[bank][:, 129 * slot:129 * (slot + 1)]
                for t in range(4):
                    PE.matmul(out,
                              dpkb[t][:, (g * FC + fc) * 128:
                                      (g * FC + fc + 1) * 128],
                              wmv[t][:, h, :],
                              start=(t == 0), stop=(t == 3))
            PE.bump(s_pe, 1)
            VE.wait(d_acc, d_acc.n)
            VE.wait(s_pe, s_pe.n)
            for bank in range(8):
                VE.tensor_copy(out=stg[:, bank, :], in_=PS[bank][:, 0:387])
            VE.bump(s_v, 1)
            PL.wait(s_v, s_v.n)
            base = (24 * rnd) * 129
            acc_op = (ALU.bypass if G == 0 else ALU.add)

            def mkd(b=base, a=acc_op):
                def go(g_):
                    rpar = PLregs["rpar"]
                    src = stg[:].rearrange("p a b -> p (a b)")
                    with g_.If_eq(rpar, 0):
                        g_.dma_start(out=sh_delta[0, :, b:b + 3096],
                                     in_=src, accum_op=a,
                                     max_dma_last_dim=2048
                                     ).then_inc(d_acc.h, 16)
                    with g_.Else():
                        g_.dma_start(out=sh_delta[1, :, b:b + 3096],
                                     in_=src, accum_op=a,
                                     max_dma_last_dim=2048
                                     ).then_inc(d_acc.h, 16)
                return go
            PL.raw(mkd())
            d_acc.n += 16
            PE.wait(s_v, s_v.n)
        if G == 0:
            ckpt("p1g0")

    PL.wait(d_acc, d_acc.n)
    ckpt("p1")
    pair_barrier()
    ckpt("p1bar")

    ar.release()
    ar.mark()
    bA = ar.alloc("bA", [128, 2064], F32)
    bB = ar.alloc("bB", [128, 2064], F32)
    bC = ar.alloc("bC", [128, 16, 128], F32)
    phase_barrier()
    for c in range(6):
        base = c * 16 * 129
        if os.environ.get("KSOLO") == "1":
            VE.memset(bA[:], 0.0)
            VE.memset(bB[:], 0.0)
            VE.drain()
            VE.bump(s_v, 1)
            PL.wait(s_v, s_v.n)
        else:
            PL.dma_start(out=bA[:], in_=sh_delta[0, :, base:base + 2064]
                         ).then_inc(d_pl, 16)
            PL.wait(d_pl, d_pl.n)
            PL.dma_start(out=bB[:], in_=sh_delta[1, :, base:base + 2064]
                         ).then_inc(d_pl, 16)
            PL.wait(d_pl, d_pl.n)
        PL.dma_start(out=bC[:], in_=mem_r[:, 16 * c:16 * (c + 1), :]
                     ).then_inc(d_pl, 16)
        PL.wait(d_pl, d_pl.n)
        VE.wait(d_pl, d_pl.n)
        VE.tensor_tensor(out=bA[:], in0=bA[:], in1=bB[:], op=ALU.add)
        VE.drain()
        bAv = bA[:].rearrange("p (s w) -> p s w", w=129)
        VE.tensor_tensor(out=mem_aug[:, 16 * c:16 * (c + 1), :],
                         in0=bAv[:, :, 0:128], in1=bC[:], op=ALU.add)
        VE.drain()
        for s in range(16):
            j = 16 * c + s
            h, fc = j // FC, j % FC
            VE.tensor_tensor(out=nrm32[:, fc, h:h + 1],
                             in0=nrm32[:, fc, h:h + 1],
                             in1=bAv[:, s, 128:129], op=ALU.add)
        VE.drain()
        VE.bump(s_v, 1)
        PL.wait(s_v, s_v.n)
    VE.tensor_copy(out=nrmcol[:], in_=nrm32[:])
    VE.drain()
    VE.bump(s_v, 1)
    if os.environ.get("KTAP") == "nm":
        PL.wait(s_v, s_v.n)
        yflat = y.rearrange("a b -> (a b)")
        PL.dma_start(out=yflat[0:128 * 12288].rearrange("(p w) -> p w", p=128),
                     in_=mem_aug[:].rearrange("p a b -> p (a b)")
                     ).then_inc(d_pl, 16)
        PL.wait(d_pl, d_pl.n)
        PL.dma_start(out=yflat[128 * 12288:128 * 12288 + 128 * 96]
                     .rearrange("(p w) -> p w", p=128),
                     in_=nrm32[:].rearrange("p a b -> p (a b)")
                     ).then_inc(d_pl, 16)
        PL.wait(d_pl, d_pl.n)
    ckpt("p15")
    ar.release()

    ar.mark()
    ostat_m = ar.alloc("ostat_m", [128, NT_OWN], F32)
    ostat_r = ar.alloc("ostat_r", [128, NT_OWN], F32)
    oscr1 = ar.alloc("oscr1", [128, NT_OWN], F32)
    oscr2 = ar.alloc("oscr2", [128, NT_OWN], F32)
    xt2b = [ar.alloc(f"xtb{i}", [128, D], F32) for i in range(2)]
    hsb2 = ar.alloc("hsb2", [128, D], BF16)
    hsT2 = [ar.alloc(f"hsT2{i}", [128, 16, 128], BF16) for i in range(2)]
    wbuf2 = [ar.alloc(f"wbuf2{i}", [128, 16, 256], BF16) for i in range(2)]
    qst = [ar.alloc(f"qst{i}", [128, D], F32) for i in range(2)]
    gatest = [ar.alloc(f"gatest{i}", [128, D], F32) for i in range(2)]
    xr2 = ar.alloc("xr2", [128, 8, 256], F32)
    dpq32 = ar.alloc("dpq32", [128, 8, FD], F32)
    dpqb = ar.alloc("dpqb", [128, 8 * FD], BF16)
    dpqT = ar.alloc("dpqT", [128, 8 * FC, 128], BF16)
    n2q = ar.alloc("n2q", [128, 16], F32)
    rsiq = ar.alloc("rsiq", [128, 16], F32)
    scaq = ar.alloc("scaq", [128, 16], F32)
    scbq = ar.alloc("scbq", [128, 16], F32)
    s1e8q = ar.alloc("s1e8q", [128, 16], F32)
    dDq = ar.alloc("dDq", [128, 16], F32)
    recDq = ar.alloc("recDq", [128, 16], F32)
    mh = ar.alloc("mh", [128, D], F32)

    xo_t = xo.rearrange("(n p) d -> n p d", p=128)
    xb_t = xbuf.ap().rearrange("(n p) d -> n p d", p=128)

    phase_barrier()
    for _sm, _vv in CACHE_T["wqg_c"]:
        PE.wait(_sm, _vv)

    T_x2 = []
    SP.dma_start(out=xt2b[0][:], in_=xo_t[0]).then_inc(d_xp[0], 16)
    T_x2.append((0, d_xp[0].n))
    for T in range(NT_OWN):
        if T + 1 < NT_OWN:
            if T >= 1:
                SP.wait(s_v, s_v.n)
            SP.dma_start(out=xt2b[(T + 1) % 2][:], in_=xo_t[T + 1]
                         ).then_inc(d_xp[(T + 1) % 2], 16)
            T_x2.append(((T + 1) % 2, d_xp[(T + 1) % 2].n))
        xt = xt2b[T % 2][:]
        VE.wait(d_xp[T_x2[T][0]], T_x2[T][1])
        VE.reduce_sum(ostat_m[:, T:T + 1], xt, axis=AX.X)
        VE.drain()
        VE.tensor_scalar(out=ostat_m[:, T:T + 1], in0=ostat_m[:, T:T + 1],
                         scalar1=1.0 / D, scalar2=None, op0=ALU.mult)
        VE.drain()
        VE.tensor_scalar(out=SCR[:], in0=xt, scalar1=ostat_m[:, T:T + 1],
                         scalar2=None, op0=ALU.subtract)
        VE.drain()
        VE.tensor_tensor(out=SCR[:], in0=SCR[:], in1=SCR[:], op=ALU.mult)
        VE.drain()
        VE.reduce_sum(ostat_r[:, T:T + 1], SCR[:], axis=AX.X)
        VE.bump(s_v, 1)
    VE.drain()
    VE.tensor_scalar(out=ostat_r[:], in0=ostat_r[:], scalar1=1.0 / D,
                     scalar2=1e-5, op0=ALU.mult, op1=ALU.add)
    VE.drain()
    emit_rsqrt(VE, ostat_r[:], ostat_r[:], oscr1[:], oscr2[:])
    VE.bump(s_v, 1)

    for G in range(8):
        VE.wait(s_a, s_a.n)
        for t in range(2):
            T = 2 * G + t
            xt = xt2b[T % 2][:]
            SP.wait(s_v, s_v.n)
            SP.dma_start(out=xt2b[T % 2][:], in_=xo_t[T]
                         ).then_inc(d_xp[T % 2], 16)
            VE.wait(d_xp[T % 2], d_xp[T % 2].n)
            VE.tensor_scalar(out=SCR[:], in0=xt,
                             scalar1=ostat_m[:, T:T + 1],
                             scalar2=ostat_r[:, T:T + 1],
                             op0=ALU.subtract, op1=ALU.mult)
            VE.tensor_copy(out=hsb2[:], in_=SCR[:])
            VE.bump(s_v, 1)
            PE.wait(s_v, s_v.n)
            for half in range(2):
                for j in range(8):
                    c = half * 8 + j
                    PE.transpose(PSb[6][:, 128 * j:128 * (j + 1)],
                                 hsb2[:, 128 * c:128 * (c + 1)], ident[:])
                PE.bump(s_pe, 1)
                AC.wait(s_pe, s_pe.n)
                AC.copy(out=hsT2[t][:, 8 * half:8 * half + 8, :]
                        .rearrange("p a b -> p (a b)"), in_=PSb[6][:, 0:1024])
                AC.bump(s_a, 1)
                PE.wait(s_a, s_a.n)
            VE.wait(s_a, s_a.n)

        for ob in range(16):
            buf = wbuf2[ob % 2]
            if G > 0 or ob > 1:
                SP.wait(s_pe, s_pe.n)
            SP.dma_start(
                out=buf[:],
                in_=wqg_c.ap()[:, 256 * ob:256 * (ob + 1)]
                .rearrange("(dc p) o -> p dc o", p=128)
                ).then_inc(d_wp[ob % 2], 16)
            PE.wait(d_wp[ob % 2], d_wp[ob % 2].n)
            for t in range(2):
                psu = PS[2 + t][:, 256 * (ob % 2):256 * (ob % 2 + 1)]
                for dc in range(16):
                    PE.matmul(psu, hsT2[t][:, dc, :], buf[:, dc, :],
                              start=(dc == 0), stop=False)
                PE.matmul(psu, onesr[:], brow_qg[:, 256 * ob:256 * (ob + 1)],
                          start=False, stop=True)
            PE.bump(s_pe, 1)
            if ob % 2 == 1:
                for t in range(2):
                    seg = ob // 2
                    VE.wait(s_pe, s_pe.n)
                    if seg < 4:
                        VE.tensor_copy(out=qst[t][:, 512 * seg:512 * (seg + 1)],
                                       in_=PS[2 + t][:])
                    else:
                        VE.tensor_copy(
                            out=gatest[t][:, 512 * (seg - 4):512 * (seg - 3)],
                            in_=PS[2 + t][:])
                    VE.bump(s_v, 1)
                PE.wait(s_v, s_v.n)

        for t in range(2):
            T = 2 * G + t
            AC.wait(s_v, s_v.n)
            for seg in range(4):
                AC.activation(out=gatest[t][:, 512 * seg:512 * (seg + 1)],
                              in_=gatest[t][:, 512 * seg:512 * (seg + 1)],
                              func=AF.Sigmoid)
            AC.bump(s_a, 1)

            for half in range(2):
                qv = qst[t][:].rearrange("p (h d) -> p h d", h=H)[
                    :, 8 * half:8 * half + 8, :]
                VE.tensor_relu(xr2[:, :, 0:128], qv)
                VE.tensor_scalar(out=xr2[:, :, 128:256], in0=qv, scalar1=-1.0,
                                 scalar2=0.0, op0=ALU.mult, op1=ALU.max)
                VE.drain()
                for j in (1, 2, 3):
                    o = 256 * (j - 1)
                    VE.tensor_tensor(out=dpq32[:, :, o + j:o + 256],
                                     in0=xr2[:, :, j:256],
                                     in1=xr2[:, :, 0:256 - j], op=ALU.mult)
                    VE.tensor_tensor(out=dpq32[:, :, o:o + j],
                                     in0=xr2[:, :, 0:j],
                                     in1=xr2[:, :, 256 - j:256], op=ALU.mult)
                VE.drain()
                VE.tensor_copy(out=dpqb[:], in_=dpq32[:]
                               .rearrange("p a b -> p (a b)"))
                VE.bump(s_v, 1)
                AC.wait(s_v, s_v.n)
                for hh in range(8):
                    AC.activation(out=SCR[:, 0:FD], in_=dpq32[:, hh, :],
                                  func=AF.Square,
                                  accum_out=n2q[:, 8 * half + hh:
                                                8 * half + hh + 1])
                AC.bump(s_a, 1)
                PE.wait(s_v, s_v.n)
                for rnd in range(6):
                    for j in range(8):
                        c = rnd * 8 + j
                        PE.transpose(PSb[6][:, 128 * j:128 * (j + 1)],
                                     dpqb[:, 128 * c:128 * (c + 1)], ident[:])
                    PE.bump(s_pe, 1)
                    AC.wait(s_pe, s_pe.n)
                    AC.copy(out=dpqT[:, 8 * rnd:8 * rnd + 8, :]
                            .rearrange("p a b -> p (a b)"),
                            in_=PSb[6][:, 0:1024])
                    AC.bump(s_a, 1)
                    PE.wait(s_a, s_a.n)
                VE.wait(s_a, s_a.n)
                emit_rsqrt(VE, rsiq[:, 8 * half:8 * half + 8],
                           n2q[:, 8 * half:8 * half + 8],
                           scaq[:, 0:8], scbq[:, 0:8], floor=1e-24)
                VE.tensor_tensor(out=s1e8q[:, 8 * half:8 * half + 8],
                                 in0=n2q[:, 8 * half:8 * half + 8],
                                 in1=rsiq[:, 8 * half:8 * half + 8],
                                 op=ALU.mult)
                VE.drain()
                VE.tensor_scalar(out=s1e8q[:, 8 * half:8 * half + 8],
                                 in0=s1e8q[:, 8 * half:8 * half + 8],
                                 scalar1=1e-8, scalar2=None, op0=ALU.mult)
                VE.drain()
                VE.bump(s_v, 1)

                if half > 0 or t > 0 or G > 0:
                    PE.wait(s_v, s_v.n)
                for hh in range(8):
                    h = 8 * half + hh
                    pnum = PS[hh // 4][:, 128 * (hh % 4):128 * (hh % 4 + 1)]
                    for fc in range(FC):
                        lt = dpqT[:, hh * FC + fc, :]
                        PE.matmul(pnum, lt,
                                  mem_aug[:, h * FC + fc, :],
                                  start=(fc == 0), stop=(fc == FC - 1))
                        PE.matmul(PS[7][:, h:h + 1], lt,
                                  nrmcol[:, fc, h:h + 1],
                                  start=(fc == 0), stop=(fc == FC - 1))
                PE.bump(s_pe, 1)
                VE.wait(s_pe, s_pe.n)
                hsl = slice(8 * half, 8 * half + 8)
                VE.tensor_tensor(out=dDq[:, hsl], in0=PS[7][:, hsl],
                                 in1=s1e8q[:, hsl], op=ALU.add)
                VE.drain()
                VE.reciprocal(recDq[:, hsl], dDq[:, hsl])
                VE.drain()
                for hh in range(8):
                    h = 8 * half + hh
                    pnum = PS[hh // 4][:, 128 * (hh % 4):128 * (hh % 4 + 1)]
                    VE.tensor_scalar(out=mh[:, 128 * h:128 * (h + 1)],
                                     in0=pnum,
                                     scalar1=recDq[:, h:h + 1],
                                     scalar2=None, op0=ALU.mult)
                VE.drain()
                VE.bump(s_v, 1)

            VE.wait(s_a, s_a.n)
            VE.tensor_tensor(out=mh[:], in0=mh[:], in1=gatest[t][:],
                             op=ALU.mult)
            VE.drain()
            VE.tensor_scalar(out=SCR[:], in0=xt2b[T % 2][:],
                             scalar1=ostat_m[:, T:T + 1],
                             scalar2=ostat_r[:, T:T + 1],
                             op0=ALU.subtract, op1=ALU.mult)
            VE.drain()
            VE.tensor_tensor(out=mh[:], in0=mh[:], in1=SCR[:], op=ALU.add)
            VE.drain()
            VE.tensor_tensor(out=mh[:], in0=mh[:], in1=xt2b[T % 2][:],
                             op=ALU.add)
            VE.drain()
            VE.reduce_sum(stats16[:, T, 0:1], mh[:], axis=AX.X)
            VE.drain()
            VE.tensor_scalar(out=stats16[:, T, 0:1], in0=stats16[:, T, 0:1],
                             scalar1=1.0 / D, scalar2=None, op0=ALU.mult)
            VE.drain()
            VE.tensor_scalar(out=SCR[:], in0=mh[:],
                             scalar1=stats16[:, T, 0:1],
                             scalar2=None, op0=ALU.subtract)
            VE.drain()
            VE.tensor_tensor(out=SCR[:], in0=SCR[:], in1=SCR[:], op=ALU.mult)
            VE.drain()
            VE.reduce_sum(stats16[:, T, 1:2], SCR[:], axis=AX.X)
            VE.bump(s_v, 1)
            SP.wait(s_v, s_v.n)
            SP.dma_start(out=xb_t[T], in_=mh[:]).then_inc(d_st, 16)
            if "x1" in taps:
                SP.dma_start(out=taps["x1"].rearrange("(n p) d -> n p d",
                                                      p=128)[T],
                             in_=mh[:]).then_inc(d_out, 16)
            VE.wait(d_st, d_st.n)

    VE.tensor_scalar(out=stats16[:, :, 1], in0=stats16[:, :, 1],
                     scalar1=1.0 / D, scalar2=1e-5,
                     op0=ALU.mult, op1=ALU.add)
    VE.drain()
    emit_rsqrt(VE, stats16[:, :, 1], stats16[:, :, 1], oscr1[:], oscr2[:])
    VE.bump(s_v, 1)
    ckpt("p2")
    ar.release()
    ar.release()

    def emit_ffn(tag, dest_t, do_stats):
        ar.mark()
        hsT3 = [ar.alloc(f"{tag}hsT{i}", [128, 16, 128], BF16)
                for i in range(4)]
        hsb3 = ar.alloc(f"{tag}hsb", [128, D], BF16)
        wguf = [ar.alloc(f"{tag}wgu{i}", [128, 16, 512], BF16)
                for i in range(2)]
        gtmp = [ar.alloc(f"{tag}gt{i}", [128, 1024], F32) for i in range(4)]
        hbuf = [ar.alloc(f"{tag}h{i}", [128, 1024], F32) for i in range(4)]
        habs = ar.alloc(f"{tag}habs", [128, 1024], F32)
        hq = [ar.alloc(f"{tag}hq{i}", [128, 256], F32) for i in range(5)]
        hbf1 = ar.alloc(f"{tag}hbf", [128, 1024], BF16)
        hbf = [hbf1] * 4
        hT = [ar.alloc(f"{tag}hT{i}", [128, 8, 128], BF16) for i in range(4)]
        wdbuf = [ar.alloc(f"{tag}wd{i}", [128, 8, 512], BF16)
                 for i in range(2)]
        dacc = [ar.alloc(f"{tag}da{i}", [128, D], F32) for i in range(4)]
        fsc1 = ar.alloc(f"{tag}fsc1", [128, NT_OWN], F32)
        fsc2 = ar.alloc(f"{tag}fsc2", [128, NT_OWN], F32)

        phase_barrier()
        for _sm, _vv in CACHE_T["wdf_c"]:
            PE.wait(_sm, _vv)
        for G in range(4):
            VE.wait(s_a, s_a.n)
            for t in range(4):
                T = 4 * G + t
                SP.wait(s_v, s_v.n)
                SP.dma_start(out=dacc[t][:], in_=xb_t[T]
                             ).then_inc(d_xp[t % 2], 16)
                VE.wait(d_xp[t % 2], d_xp[t % 2].n)
                VE.tensor_scalar(out=SCR[:], in0=dacc[t][:],
                                 scalar1=stats16[:, T, 0:1],
                                 scalar2=stats16[:, T, 1:2],
                                 op0=ALU.subtract, op1=ALU.mult)
                VE.tensor_copy(out=hsb3[:], in_=SCR[:])
                VE.bump(s_v, 1)
                PE.wait(s_v, s_v.n)
                for half in range(2):
                    for j in range(8):
                        c = half * 8 + j
                        PE.transpose(PSb[6][:, 128 * j:128 * (j + 1)],
                                     hsb3[:, 128 * c:128 * (c + 1)],
                                     ident[:])
                    PE.bump(s_pe, 1)
                    AC.wait(s_pe, s_pe.n)
                    AC.copy(out=hsT3[t][:, 8 * half:8 * half + 8, :]
                            .rearrange("p a b -> p (a b)"),
                            in_=PSb[6][:, 0:1024])
                    AC.bump(s_a, 1)
                    PE.wait(s_a, s_a.n)
                VE.wait(s_a, s_a.n)

            def w_issue(ps):
                if ps is None:
                    return None
                if ps[0] == "gu":
                    _, IG_, gu_, p2_ = ps
                    slot = (IG_ * 4 + gu_ * 2 + p2_) % 2
                    wsrc = wgf_c if gu_ == 0 else wuf_c
                    cols = slice(IG_ * 1024 + 512 * p2_,
                                 IG_ * 1024 + 512 * (p2_ + 1))
                    SP.wait(s_pe, s_pe.n)
                    SP.dma_start(
                        out=wguf[slot][:],
                        in_=wsrc.ap()[:, cols]
                        .rearrange("(dc p) o -> p dc o", p=128)
                        ).then_inc(d_wp[slot], 16)
                    return (slot, d_wp[slot].n)
                _, IG_, dq_ = ps
                slot = (IG_ * 4 + dq_) % 2
                i0_ = IG_ * 1024
                SP.wait(s_pe, s_pe.n)
                SP.dma_start(
                    out=wdbuf[slot][:],
                    in_=wdf_c.ap()[i0_:i0_ + 1024,
                                   512 * dq_:512 * (dq_ + 1)]
                    .rearrange("(a p) b -> p a b", p=128)
                    ).then_inc(d_wp[slot], 16)
                return (slot, d_wp[slot].n)

            sched = []
            for IG in range(8):
                for gu in range(2):
                    for p2 in range(2):
                        sched.append(("gu", IG, gu, p2))
                for dq in range(4):
                    sched.append(("dn", IG, dq))
            pend = {0: w_issue(sched[0])}
            si = 0
            for IG in range(8):
                i0 = IG * 1024
                for gu in range(2):
                    for p2 in range(2):
                        buf = wguf[pend[si][0]]
                        if si + 1 < len(sched):
                            pend[si + 1] = w_issue(sched[si + 1])
                        PE.wait(d_wp[pend[si][0]], pend[si][1])
                        si += 1
                        PE.wait(s_v, s_v.n)
                        PE.wait(s_a, s_a.n)
                        for t in range(4):
                            psu = PS[2 + t][:]
                            for dc in range(16):
                                PE.matmul(psu, hsT3[t][:, dc, :],
                                          buf[:, dc, :],
                                          start=(dc == 0), stop=(dc == 15))
                        PE.bump(s_pe, 1)
                        for t in range(4):
                            if gu == 0:
                                AC.wait(s_pe, s_pe.n)
                                AC.activation(
                                    out=gtmp[t][:, 512 * p2:512 * (p2 + 1)],
                                    in_=PS[2 + t][:], func=AF.Silu)
                                AC.bump(s_a, 1)
                            else:
                                VE.wait(s_pe, s_pe.n)
                                VE.wait(s_a, s_a.n)
                                VE.tensor_tensor(
                                    out=hbuf[t][:, 512 * p2:512 * (p2 + 1)],
                                    in0=PS[2 + t][:],
                                    in1=gtmp[t][:, 512 * p2:512 * (p2 + 1)],
                                    op=ALU.mult)
                                VE.bump(s_v, 1)

                for t in range(4):
                    h = hbuf[t][:]
                    hg = h.rearrange("p (g four) -> p g four", four=4)
                    VE.tensor_scalar(out=habs[:], in0=h, scalar1=-1.0,
                                     scalar2=None, op0=ALU.mult)
                    VE.drain()
                    VE.tensor_tensor(out=habs[:], in0=habs[:], in1=h,
                                     op=ALU.max)
                    VE.drain()
                    ag = habs[:].rearrange("p (g four) -> p g four", four=4)
                    VE.tensor_tensor(out=hq[0][:], in0=ag[:, :, 0],
                                     in1=ag[:, :, 1], op=ALU.max)
                    VE.tensor_tensor(out=hq[1][:], in0=ag[:, :, 0],
                                     in1=ag[:, :, 1], op=ALU.min)
                    VE.tensor_tensor(out=hq[2][:], in0=ag[:, :, 2],
                                     in1=ag[:, :, 3], op=ALU.max)
                    VE.tensor_tensor(out=hq[3][:], in0=ag[:, :, 2],
                                     in1=ag[:, :, 3], op=ALU.min)
                    VE.drain()
                    VE.tensor_tensor(out=hq[0][:], in0=hq[0][:], in1=hq[2][:],
                                     op=ALU.min)
                    VE.tensor_tensor(out=hq[1][:], in0=hq[1][:], in1=hq[3][:],
                                     op=ALU.max)
                    VE.drain()
                    VE.tensor_tensor(out=hq[4][:], in0=hq[0][:], in1=hq[1][:],
                                     op=ALU.max)
                    VE.drain()
                    hbg = hbf[t][:].rearrange("p (g four) -> p g four",
                                              four=4)
                    for q in range(4):
                        VE.tensor_tensor(out=hq[q][:], in0=ag[:, :, q],
                                         in1=hq[4][:], op=ALU.is_ge)
                    VE.drain()
                    VE.wait(s_a, s_a.n)
                    for q in range(4):
                        VE.tensor_tensor(out=hbg[:, :, q], in0=hg[:, :, q],
                                         in1=hq[q][:], op=ALU.mult)
                    VE.bump(s_v, 1)
                    PE.wait(s_v, s_v.n)
                    for j in range(8):
                        PE.transpose(PSb[6][:, 128 * j:128 * (j + 1)],
                                     hbf[t][:, 128 * j:128 * (j + 1)],
                                     ident[:])
                    PE.bump(s_pe, 1)
                    AC.wait(s_pe, s_pe.n)
                    AC.copy(out=hT[t][:].rearrange("p a b -> p (a b)"),
                            in_=PSb[6][:, 0:1024])
                    AC.bump(s_a, 1)
                    PE.wait(s_a, s_a.n)

                for dq in range(4):
                    buf = wdbuf[pend[si][0]]
                    if si + 1 < len(sched):
                        pend[si + 1] = w_issue(sched[si + 1])
                    PE.wait(d_wp[pend[si][0]], pend[si][1])
                    si += 1
                    if IG > 0 or dq > 0:
                        PE.wait(s_v, s_v.n)
                    for t in range(4):
                        for ic in range(8):
                            PE.matmul(PS[t][:], hT[t][:, ic, :],
                                      buf[:, ic, :],
                                      start=(ic == 0), stop=(ic == 7))
                    PE.bump(s_pe, 1)
                    VE.wait(s_pe, s_pe.n)
                    for t in range(4):
                        VE.tensor_tensor(
                            out=dacc[t][:, 512 * dq:512 * (dq + 1)],
                            in0=PS[t][:],
                            in1=dacc[t][:, 512 * dq:512 * (dq + 1)],
                            op=ALU.add)
                    VE.bump(s_v, 1)

            for t in range(4):
                T = 4 * G + t
                VE.reduce_sum(stats16[:, T, 0:1], dacc[t][:], axis=AX.X)
                VE.drain()
                VE.tensor_scalar(out=stats16[:, T, 0:1],
                                 in0=stats16[:, T, 0:1],
                                 scalar1=1.0 / D, scalar2=None, op0=ALU.mult)
                VE.drain()
                VE.tensor_scalar(out=SCR[:], in0=dacc[t][:],
                                 scalar1=stats16[:, T, 0:1],
                                 scalar2=None, op0=ALU.subtract)
                VE.drain()
                VE.tensor_tensor(out=SCR[:], in0=SCR[:], in1=SCR[:],
                                 op=ALU.mult)
                VE.drain()
                VE.reduce_sum(stats16[:, T, 1:2], SCR[:], axis=AX.X)
                VE.bump(s_v, 1)
                SP.wait(s_v, s_v.n)
                SP.dma_start(out=dest_t[T], in_=dacc[t][:]).then_inc(d_st, 16)
            SP.wait(d_st, d_st.n)
            VE.wait(d_st, d_st.n)
        if do_stats:
            VE.tensor_scalar(out=stats16[:, :, 1],
                             in0=stats16[:, :, 1],
                             scalar1=1.0 / D, scalar2=1e-5,
                             op0=ALU.mult, op1=ALU.add)
            VE.drain()
            emit_rsqrt(VE, stats16[:, :, 1], stats16[:, :, 1],
                       fsc1[:], fsc2[:])
            VE.bump(s_v, 1)
        ar.release()

    emit_ffn("f1", xb_t, True)
    ckpt("p3")

    ar.mark()
    cosq_sb = ar.alloc("cosq_sb", [128, 16, 32], F32)
    sinq_sb = ar.alloc("sinq_sb", [128, 16, 32], F32)
    cosk_sb = ar.alloc("cosk_sb", [128, 16, 32], F32)
    sink_sb = ar.alloc("sink_sb", [128, 16, 32], F32)
    sublnw_sb = ar.alloc("sublnw_sb", [128, 128], F32)
    lam_sb = ar.alloc("lam_sb", [128, 256], F32)
    neglam = ar.alloc("neglam", [128, 1], F32)
    lsc1 = ar.alloc("lsc1", [128, 1], F32)
    lsc2 = ar.alloc("lsc2", [128, 1], F32)

    phase_barrier()
    for _sm, _vv in CACHE_T["wo_c"]:
        PE.wait(_sm, _vv)

    SP.dma_start(out=cosq_sb[:], in_=cosq.rearrange("(T p) i -> p T i", p=128)
                 ).then_inc(d_x, 16)
    SP.dma_start(out=sinq_sb[:], in_=sinq.rearrange("(T p) i -> p T i", p=128)
                 ).then_inc(d_x, 16)
    SP.dma_start(out=cosk_sb[:], in_=cosk.rearrange("(T p) i -> p T i", p=128)
                 ).then_inc(d_x, 16)
    SP.dma_start(out=sink_sb[:], in_=sink.rearrange("(T p) i -> p T i", p=128)
                 ).then_inc(d_x, 16)
    SP.dma_start(out=sublnw_sb[:], in_=sublnw_r).then_inc(d_x, 16)
    SP.dma_start(out=lam_sb[:], in_=lam_in).then_inc(d_x, 16)
    PH4_CONST = d_x.n

    VE.wait(d_x, PH4_CONST)
    VE.tensor_tensor(out=lam_sb[:, 0:64], in0=lam_sb[:, 0:64],
                     in1=lam_sb[:, 128:192], op=ALU.mult)
    VE.drain()
    VE.reduce_sum(lsc1[:], lam_sb[:, 0:64], axis=AX.X)
    VE.drain()
    VE.tensor_tensor(out=lam_sb[:, 64:128], in0=lam_sb[:, 64:128],
                     in1=lam_sb[:, 192:256], op=ALU.mult)
    VE.drain()
    VE.reduce_sum(lsc2[:], lam_sb[:, 64:128], axis=AX.X)
    VE.bump(s_v, 1)
    AC.wait(s_v, s_v.n)
    AC.activation(out=lsc1[:], in_=lsc1[:], func=AF.Exp)
    AC.activation(out=lsc2[:], in_=lsc2[:], func=AF.Exp)
    AC.bump(s_a, 1)
    VE.wait(s_a, s_a.n)
    VE.tensor_tensor(out=neglam[:], in0=lsc2[:], in1=lsc1[:], op=ALU.subtract)
    VE.drain()
    VE.tensor_scalar(out=neglam[:], in0=neglam[:], scalar1=-LAMBDA_INIT,
                     scalar2=None, op0=ALU.add)
    VE.bump(s_v, 1)

    ar.mark()
    xt4 = [ar.alloc(f"a4x{i}", [128, D], F32) for i in range(2)]
    hsb4 = ar.alloc("a4hsb", [128, D], BF16)
    hsT4 = [ar.alloc(f"a4hsT{i}", [128, 16, 128], BF16) for i in range(2)]
    wbuf4 = [ar.alloc(f"a4w{i}", [128, 16, 512], BF16) for i in range(2)]
    qkv32 = [ar.alloc(f"a4qkv{i}", [128, 2, D], F32) for i in range(2)]
    vbf4 = [ar.alloc(f"a4vbf{i}", [128, D], BF16) for i in range(2)]
    rp = [ar.alloc(f"a4rp{i}", [128, 1024], F32) for i in range(4)]
    qkbf = ar.alloc("a4qkbf", [128, D], BF16)
    tstg = [ar.alloc(f"a4st{i}", [128, 8, 128], BF16) for i in range(2)]
    d_sh = [mksem("d_sh0"), mksem("d_sh1")]

    def bcast32(tab, T):
        return tab[:, T:T + 1, :].to_broadcast((128, 32, 32))

    for G in range(8):
        VE.wait(s_a, s_a.n)
        for t in range(2):
            T = 2 * G + t
            SP.wait(s_v, s_v.n)
            SP.dma_start(out=xt4[t][:], in_=xb_t[T]).then_inc(d_xp[t], 16)
            VE.wait(d_xp[t], d_xp[t].n)
            VE.tensor_scalar(out=SCR[:], in0=xt4[t][:],
                             scalar1=stats16[:, T, 0:1],
                             scalar2=stats16[:, T, 1:2],
                             op0=ALU.subtract, op1=ALU.mult)
            VE.tensor_copy(out=hsb4[:], in_=SCR[:])
            VE.bump(s_v, 1)
            PE.wait(s_v, s_v.n)
            for half in range(2):
                for j in range(8):
                    c = half * 8 + j
                    PE.transpose(PSb[6][:, 128 * j:128 * (j + 1)],
                                 hsb4[:, 128 * c:128 * (c + 1)], ident[:])
                PE.bump(s_pe, 1)
                AC.wait(s_pe, s_pe.n)
                AC.copy(out=hsT4[t][:, 8 * half:8 * half + 8, :]
                        .rearrange("p a b -> p (a b)"), in_=PSb[6][:, 0:1024])
                AC.bump(s_a, 1)
                PE.wait(s_a, s_a.n)
            VE.wait(s_a, s_a.n)

        for ob in range(12):
            buf = wbuf4[ob % 2]
            if G > 0 or ob > 1:
                SP.wait(s_pe, s_pe.n)
            SP.dma_start(
                out=buf[:],
                in_=wqkvb_c.ap()[:, 512 * ob:512 * (ob + 1)]
                .rearrange("(dc p) o -> p dc o", p=128)
                ).then_inc(d_wp[ob % 2], 16)
            PE.wait(d_wp[ob % 2], d_wp[ob % 2].n)
            PE.wait(s_v, s_v.n)
            for t in range(2):
                psu = PS[2 + t][:]
                for dc in range(16):
                    PE.matmul(psu, hsT4[t][:, dc, :], buf[:, dc, :],
                              start=(dc == 0), stop=(dc == 15))
            PE.bump(s_pe, 1)
            for t in range(2):
                VE.wait(s_pe, s_pe.n)
                part, off = ob // 4, 512 * (ob % 4)
                if part < 2:
                    VE.tensor_copy(out=qkv32[t][:, part, off:off + 512],
                                   in_=PS[2 + t][:])
                else:
                    VE.tensor_copy(out=vbf4[t][:, off:off + 512],
                                   in_=PS[2 + t][:])
                VE.bump(s_v, 1)
            PE.wait(s_v, s_v.n)

        for t in range(2):
            T = 2 * G + t
            for part, (ctab, stab) in ((0, (cosq_sb, sinq_sb)),
                                       (1, (cosk_sb, sink_sb))):
                pv = qkv32[t][:, part, :].rearrange(
                    "p (h i two) -> p h i two", i=32, two=2)
                ev, od = pv[:, :, :, 0], pv[:, :, :, 1]
                cB, sB = bcast32(ctab, T), bcast32(stab, T)
                r0 = rp[0][:].rearrange("p (h i) -> p h i", i=32)
                r1 = rp[1][:].rearrange("p (h i) -> p h i", i=32)
                r2 = rp[2][:].rearrange("p (h i) -> p h i", i=32)
                r3 = rp[3][:].rearrange("p (h i) -> p h i", i=32)
                VE.tensor_tensor(out=r0, in0=ev, in1=cB, op=ALU.mult)
                VE.tensor_tensor(out=r1, in0=od, in1=sB, op=ALU.mult)
                VE.tensor_tensor(out=r2, in0=ev, in1=sB, op=ALU.mult)
                VE.tensor_tensor(out=r3, in0=od, in1=cB, op=ALU.mult)
                VE.drain()
                obv = qkbf[:].rearrange("p (h i two) -> p h i two",
                                        i=32, two=2)
                VE.tensor_tensor(out=obv[:, :, :, 0], in0=r0, in1=r1,
                                 op=ALU.subtract)
                VE.tensor_tensor(out=obv[:, :, :, 1], in0=r2, in1=r3,
                                 op=ALU.add)
                VE.bump(s_v, 1)
                dst_d = qT_d if part == 0 else kT_d
                sh_plane = 0 if part == 0 else 1
                PE.wait(s_v, s_v.n)
                for half in range(2):
                    stg4 = tstg[half]
                    SP.wait(d_sh[half], d_sh[half].n)
                    for j in range(8):
                        c = half * 8 + j
                        PE.transpose(PSb[6][:, 128 * j:128 * (j + 1)],
                                     qkbf[:, 128 * c:128 * (c + 1)],
                                     ident[:])
                    PE.bump(s_pe, 1)
                    AC.wait(s_pe, s_pe.n)
                    AC.wait(d_sh[half], d_sh[half].n)
                    AC.copy(out=stg4[:].rearrange("p a b -> p (a b)"),
                            in_=PSb[6][:, 0:1024])
                    AC.bump(s_a, 1)
                    PE.wait(s_a, s_a.n)
                    SP.wait(s_a, s_a.n)
                    if half == 0:
                        SP.dma_start(
                            out=dst_d.ap()
                            .rearrange("(c p) t -> p c t", p=128)
                            [:, :, 128 * T:128 * (T + 1)],
                            in_=stg4[:]).then_inc(d_sh[half], 16)
                    else:
                        def mk_sh(stg_=stg4, pl_=sh_plane, T_=T, hf_=half):
                            def go(g_):
                                rpar = PLregs["rpar"]
                                for par in range(2):
                                    ctx = (g_.If_eq(rpar, 0) if par == 0
                                           else g_.Else())
                                    with ctx:
                                        g_.dma_start(
                                            out=sh_qkv[par, pl_]
                                            .rearrange("(c p) t -> p c t",
                                                       p=128)
                                            [:, :, 128 * T_:128 * (T_ + 1)],
                                            in_=stg_[:]
                                            ).then_inc(d_sh[hf_].h, 16)
                            return go
                        PL.wait(s_a, s_a.n)
                        PL.raw(mk_sh())
                        d_sh[half].n += 16
            SP.wait(d_sh[0], d_sh[0].n)
            SP.dma_start(out=v_d.ap()[128 * T:128 * (T + 1), :],
                         in_=vbf4[t][:, 0:1024]).then_inc(d_sh[0], 16)

            def mk_shv(T_=T):
                def go(g_):
                    rpar = PLregs["rpar"]
                    for par in range(2):
                        ctx = (g_.If_eq(rpar, 0) if par == 0 else g_.Else())
                        with ctx:
                            g_.dma_start(
                                out=sh_qkv[par, 2].rearrange("a b -> (a b)")
                                [1024 * 128 * T_:1024 * 128 * (T_ + 1)]
                                .rearrange("(p f) -> p f", p=128),
                                in_=vbf4[T_ % 2][:, 1024:2048]
                                ).then_inc(d_sh[1].h, 16)
                return go
            PL.wait(d_sh[1], d_sh[1].n)
            PL.raw(mk_shv())
            d_sh[1].n += 16
            VE.wait(d_sh[0], d_sh[0].n)
            VE.wait(d_sh[1], d_sh[1].n)
    SP.wait(d_sh[0], d_sh[0].n)
    PL.wait(d_sh[1], d_sh[1].n)
    ar.release()
    ckpt("p4a")
    pair_barrier()
    phase_barrier()

    ar.mark()
    kTs = ar.alloc("kTs", [128, 8, SEQ], BF16)
    vas = ar.alloc("vas", [128, 32, 1032], BF16)
    qTt = [ar.alloc(f"qTt{i}", [128, 8, 512], BF16) for i in range(2)]
    eTs = ar.alloc("eTs", [128, 32, 512], BF16)
    pstg = ar.alloc("pstg", [128, 512], BF16)
    d_m = [mksem("d_m0"), mksem("d_m1")]
    d_at = mksem("d_at")
    d_ao = mksem("d_ao")
    lbuf = ar.alloc("lbuf", [128, 8], F32)
    rlbuf = ar.alloc("rlbuf", [128, 8], F32)
    rn1 = ar.alloc("rn1", [128, 4], F32)
    ssb = ar.alloc("ssb", [128, 4], F32)
    at32 = [ar.alloc(f"at32_{i}", [128, 128], F32) for i in range(4)]
    abf = ar.alloc("abf", [128, 128], BF16)
    ar.mark()
    mcw = ar.alloc("mcw", [128, 16, 4, 8], F32)
    mcw2 = ar.alloc("mcw2", [128, 16, 4, 4], F32)
    mcw3 = ar.alloc("mcw3", [128, 16, 4, 2], F32)
    mcw4 = ar.alloc("mcw4", [128, 64], F32)
    mneg = ar.alloc("mneg", [128, 64], BF16)
    mstg = ar.alloc("mstg", [64, 128], F32)
    d_q = [mksem("d_q0"), mksem("d_q1")]
    d_qp = [mksem("d_qp0"), mksem("d_qp1")]
    d_kv = mksem("d_kv")
    d_kvp = mksem("d_kvp")

    SP.dma_start(out=kTs[:, :, 0:ROWS],
                 in_=kT_d.ap().rearrange("(c p) t -> p c t", p=128)
                 ).then_inc(d_kv, 16)

    def mk_peer_loads():
        def go(g_):
            rpar = PLregs["rpar"]
            for par in range(2):
                src_par = 1 - par
                ctx = (g_.If_eq(rpar, 0) if par == 0 else g_.Else())
                with ctx:
                    g_.dma_start(
                        out=kTs[:, :, ROWS:SEQ],
                        in_=sh_qkv[src_par, 1]
                        .rearrange("(c p) t -> p c t", p=128)
                        ).then_inc(d_kvp.h, 16)
                    vsrc = (sh_qkv[src_par, 2].rearrange("a b -> (a b)")
                            .rearrange("(t f) -> t f", f=1024))
                    for tc in range(16):
                        g_.dma_start(
                            out=vas[:, 16 + tc, :]
                            .rearrange("p (v d) -> p v d", d=129)
                            [:, :, 0:128],
                            in_=vsrc[128 * tc:128 * (tc + 1), :]
                            .rearrange("p (v d) -> p v d", d=128)
                            ).then_inc(d_kvp.h, 16)
        return go

    VE.memset(vas[:], 1.0)
    VE.drain()
    VE.bump(s_v, 1)
    SP.wait(s_v, s_v.n)
    for tc in range(16):
        SP.dma_start(
            out=vas[:, tc, :].rearrange("p (v d) -> p v d", d=129)
            [:, :, 0:128],
            in_=v_d.ap()[128 * tc:128 * (tc + 1), :]
            .rearrange("p (v d) -> p v d", d=128)).then_inc(d_kv, 16)
    PL.wait(s_v, s_v.n)
    PL.raw(mk_peer_loads())
    d_kvp.n += 16 * 17
    KV_READY = (d_kv.n, d_kvp.n)


    m_blk = 0
    m_bank_free = {b: None for b in range(6)}
    for Qb in range(8):
        qb_buf = qTt[Qb % 2]
        if Qb < 4:
            SP.wait(d_q[Qb % 2], d_q[Qb % 2].n)
            SP.wait(d_qp[Qb % 2], d_qp[Qb % 2].n)
            SP.wait(s_pe, s_pe.n)
            SP.dma_start(
                out=qb_buf[:],
                in_=qT_d.ap().rearrange("(c p) t -> p c t", p=128)
                [:, :, 512 * Qb:512 * (Qb + 1)]
                ).then_inc(d_q[Qb % 2], 16)
        else:
            def mk_qload_m(dst=qb_buf, Qb_=Qb, par_i=(Qb % 2)):
                def go(g_):
                    rpar = PLregs["rpar"]
                    for par in range(2):
                        src_par = 1 - par
                        ctx = (g_.If_eq(rpar, 0) if par == 0 else g_.Else())
                        with ctx:
                            g_.dma_start(
                                out=dst[:],
                                in_=sh_qkv[src_par, 0]
                                .rearrange("(c p) t -> p c t", p=128)
                                [:, :, 512 * (Qb_ - 4):512 * (Qb_ - 3)]
                                ).then_inc(d_qp[par_i].h, 16)
                return go
            PL.wait(d_qp[Qb % 2], d_qp[Qb % 2].n)
            PL.wait(s_pe, s_pe.n)
            PL.raw(mk_qload_m())
            d_qp[Qb % 2].n += 16
        if Qb < 4:
            PE.wait(d_q[Qb % 2], d_q[Qb % 2].n)
        else:
            PE.wait(d_qp[Qb % 2], d_qp[Qb % 2].n)
        if Qb == 0:
            PE.wait(d_kv, KV_READY[0])
            PE.wait(d_kvp, KV_READY[1])
        for hh in range(16):
            chunk, base = hh // 2, 64 * (hh % 2)
            for j in range(4):
                lhsT = qb_buf[base:base + 64, chunk, 128 * j:128 * (j + 1)]
                for sb2 in range(8):
                    mb_ = m_blk % 6
                    if m_bank_free[mb_] is not None:
                        PE.wait(s_v, m_bank_free[mb_])
                    PE.matmul(PS[mb_][:], lhsT,
                              kTs[base:base + 64, chunk,
                                  512 * sb2:512 * (sb2 + 1)],
                              start=True, stop=True)
                    PE.bump(s_pe, 1)
                    VE.wait(s_pe, s_pe.n)
                    VE.reduce_max(mcw[:, hh, j, sb2:sb2 + 1], PS[mb_][:],
                                  axis=AX.X)
                    VE.bump(s_v, 1)
                    m_bank_free[mb_] = s_v.n
                    m_blk += 1
        VE.tensor_tensor(out=mcw2[:], in0=mcw[:, :, :, 0:4],
                         in1=mcw[:, :, :, 4:8], op=ALU.max)
        VE.tensor_tensor(out=mcw3[:], in0=mcw2[:, :, :, 0:2],
                         in1=mcw2[:, :, :, 2:4], op=ALU.max)
        VE.tensor_tensor(out=mcw4[:].rearrange("p (a b c) -> p a b c",
                                               a=16, b=4),
                         in0=mcw3[:, :, :, 0:1],
                         in1=mcw3[:, :, :, 1:2], op=ALU.max)
        VE.tensor_scalar(out=mneg[:], in0=mcw4[:],
                         scalar1=-1.0, scalar2=None, op0=ALU.mult)
        VE.bump(s_v, 1)
        PE.wait(s_v, s_v.n)
        PE.wait(s_a, s_a.n)
        PE.transpose(PSb[6][0:64, 0:128], mneg[:], ident[:])
        PE.bump(s_pe, 1)
        AC.wait(s_pe, s_pe.n)
        AC.wait(d_m[0], d_m[0].n)
        AC.copy(out=mstg[:], in_=PSb[6][0:64, 0:128])
        AC.bump(s_a, 1)
        PE.wait(s_a, s_a.n)
        SP.wait(s_a, s_a.n)
        for hh in range(16):
            SP.dma_start(out=md_d.ap()[hh, 512 * Qb:512 * (Qb + 1)]
                         .rearrange("(j t) -> j t", j=4),
                         in_=mstg[4 * hh:4 * hh + 4, :]).then_inc(d_m[0], 16)
    if os.environ.get("KTAP") == "md":
        SP.wait(d_m[0], d_m[0].n)
        PL.wait(d_m[0], d_m[0].n)
        PL.dma_start(out=y.rearrange("a b -> (a b)")[0:16 * SEQ]
                     .rearrange("(p w) -> p w", p=16),
                     in_=md_d.ap()).then_inc(d_pl, 16)
        PL.wait(d_pl, d_pl.n)
    ckpt("p4m")
    ar.release()
    nM = [ar.alloc(f"nM{i}", [128, 2, 512], F32) for i in range(2)]
    sscr = [PS[4], PS[5]]

    sv_bank = [None, None]
    exp_free = [None, None]
    for vh in range(8):
        c_hh = vh
        for Qb in range(8):
            qb_buf = qTt[Qb % 2]
            if Qb < 4:
                SP.wait(d_q[Qb % 2], d_q[Qb % 2].n)
                SP.wait(d_qp[Qb % 2], d_qp[Qb % 2].n)
                SP.wait(s_pe, s_pe.n)
                SP.dma_start(
                    out=qb_buf[:],
                    in_=qT_d.ap().rearrange("(c p) t -> p c t", p=128)
                    [:, :, 512 * Qb:512 * (Qb + 1)]
                    ).then_inc(d_q[Qb % 2], 16)
            else:
                def mk_qload(dst=qb_buf, Qb_=Qb, par_i=(Qb % 2)):
                    def go(g_):
                        rpar = PLregs["rpar"]
                        for par in range(2):
                            src_par = 1 - par
                            ctx = (g_.If_eq(rpar, 0) if par == 0
                                   else g_.Else())
                            with ctx:
                                g_.dma_start(
                                    out=dst[:],
                                    in_=sh_qkv[src_par, 0]
                                    .rearrange("(c p) t -> p c t", p=128)
                                    [:, :, 512 * (Qb_ - 4):512 * (Qb_ - 3)]
                                    ).then_inc(d_qp[par_i].h, 16)
                    return go
                PL.wait(d_qp[Qb % 2], d_qp[Qb % 2].n)
                PL.wait(s_pe, s_pe.n)
                PL.raw(mk_qload())
                d_qp[Qb % 2].n += 16
            if Qb < 4:
                PE.wait(d_q[Qb % 2], d_q[Qb % 2].n)
            else:
                PE.wait(d_qp[Qb % 2], d_qp[Qb % 2].n)
            if vh == 0 and Qb == 0:
                PE.wait(d_kv, KV_READY[0])
                PE.wait(d_kvp, KV_READY[1])
            PE.wait(s_v, s_v.n)

            SP.wait(d_m[Qb % 2], d_m[Qb % 2].n)
            SP.wait(s_v, s_v.n)
            for hh in range(2):
                SP.dma_start(out=nM[Qb % 2][:, hh, :],
                             in_=md_d.ap()[2 * vh + hh,
                                           512 * Qb:512 * (Qb + 1)][None, :]
                             .to_broadcast((128, 512))
                             ).then_inc(d_m[Qb % 2], 16)
            VE.wait(d_m[Qb % 2], d_m[Qb % 2].n)
            for hh in range(2):
                base = 64 * hh
                for sc in range(32):
                    sb_ = 6 + (sc % 2)
                    if sv_bank[sc % 2] is not None:
                        PE.wait(s_v, sv_bank[sc % 2])
                    PE.matmul(PS[sb_][:],
                              kTs[base:base + 64, c_hh,
                                  128 * sc:128 * (sc + 1)],
                              qb_buf[base:base + 64, c_hh, :],
                              start=True, stop=True)
                    PE.bump(s_pe, 1)
                    VE.wait(s_pe, s_pe.n)
                    if exp_free[sc % 2] is not None:
                        VE.wait(s_a, exp_free[sc % 2])
                    VE.tensor_tensor(out=sscr[sc % 2][:], in0=PS[sb_][:],
                                     in1=nM[Qb % 2][:, hh, :], op=ALU.add)
                    VE.bump(s_v, 1)
                    sv_bank[sc % 2] = s_v.n
                    AC.wait(s_v, s_v.n)
                    AC.activation(out=eTs[:, sc, :], in_=sscr[sc % 2][:],
                                  func=AF.Exp)
                    AC.bump(s_a, 1)
                    exp_free[sc % 2] = s_a.n
                EXP_DONE = s_a.n
                PE.wait(s_a, EXP_DONE)
                for sub in range(4):
                    idx = hh * 4 + sub
                    bank, off = idx // 3, 129 * (idx % 3)
                    for sc in range(32):
                        PE.matmul(PS[bank][:, off:off + 129],
                                  eTs[:, sc, 128 * sub:128 * (sub + 1)],
                                  vas[:, sc, 129 * vh:129 * (vh + 1)],
                                  start=(sc == 0), stop=(sc == 31))
                PE.bump(s_pe, 1)
                AC.wait(s_pe, s_pe.n)

            VE.wait(s_pe, s_pe.n)
            for idx in range(8):
                bank, off = idx // 3, 129 * (idx % 3)
                VE.tensor_copy(out=lbuf[:, idx:idx + 1],
                               in_=PS[bank][:, off + 128:off + 129])
            VE.drain()
            VE.reciprocal(rlbuf[:], lbuf[:])
            VE.drain()
            VE.wait(s_v, s_v.n)
            VE.tensor_scalar(out=rn1[:], in0=rlbuf[:, 4:8],
                             scalar1=neglam[:], scalar2=None, op0=ALU.mult)
            VE.drain()
            for sub in range(4):
                i0 = sub
                i1 = 4 + sub
                b0, o0 = i0 // 3, 129 * (i0 % 3)
                b1, o1 = i1 // 3, 129 * (i1 % 3)
                VE.tensor_scalar(out=at32[sub][:],
                                 in0=PS[b0][:, o0:o0 + 128],
                                 scalar1=rlbuf[:, sub:sub + 1],
                                 scalar2=None, op0=ALU.mult)
                VE.drain()
                VE.scalar_tensor_tensor(out=at32[sub][:],
                                        in0=PS[b1][:, o1:o1 + 128],
                                        scalar=rn1[:, sub:sub + 1],
                                        in1=at32[sub][:],
                                        op0=ALU.mult, op1=ALU.add)
                VE.bump(s_v, 1)
            AC.wait(s_v, s_v.n)
            for sub in range(4):
                AC.activation(out=SCR[:, 0:128], in_=at32[sub][:],
                              func=AF.Square,
                              accum_out=ssb[:, sub:sub + 1])
            AC.bump(s_a, 1)
            VE.wait(s_a, s_a.n)
            VE.tensor_scalar(out=ssb[:], in0=ssb[:], scalar1=1.0 / 128.0,
                             scalar2=1e-5, op0=ALU.mult, op1=ALU.add)
            VE.bump(s_v, 1)
            AC.wait(s_v, s_v.n)
            AC.activation(out=ssb[:], in_=ssb[:], func=AF.Ln)
            AC.drain()
            AC.activation(out=ssb[:], in_=ssb[:], func=AF.Exp, scale=-0.5)
            AC.bump(s_a, 1)
            VE.wait(s_a, s_a.n)
            for sub in range(4):
                qbg = 4 * Qb + sub
                VE.scalar_tensor_tensor(out=at32[sub][:], in0=at32[sub][:],
                                        scalar=ssb[:, sub:sub + 1],
                                        in1=sublnw_sb[:],
                                        op0=ALU.mult, op1=ALU.mult)
                VE.drain()
                VE.tensor_copy(out=abf[:], in_=at32[sub][:])
                VE.bump(s_v, 1)
                PE.wait(s_v, s_v.n)
                PE.transpose(PSb[3][:, 0:128], abf[:], ident[:])
                PE.bump(s_pe, 1)
                AC.wait(s_pe, s_pe.n)
                AC.copy(out=pstg[:, 128 * sub:128 * (sub + 1)],
                        in_=PSb[3][:, 0:128])
                AC.bump(s_a, 1)
                PE.wait(s_a, s_a.n)
                VE.wait(s_a, s_a.n)
            if Qb < 4:
                SP.wait(s_a, s_a.n)
                SP.dma_start(out=at_d.ap()[vh, :, 512 * Qb:512 * (Qb + 1)],
                             in_=pstg[:]).then_inc(d_at, 16)
                AC.wait(d_at, d_at.n)
            else:
                def mk_attn_out(vh_=vh, Qb_=Qb):
                    def go(g_):
                        rpar = PLregs["rpar"]
                        for par in range(2):
                            ctx = (g_.If_eq(rpar, 0) if par == 0
                                   else g_.Else())
                            with ctx:
                                g_.dma_start(
                                    out=sh_attn[par, vh_, :,
                                                512 * (Qb_ - 4):
                                                512 * (Qb_ - 3)],
                                    in_=pstg[:]).then_inc(d_ao.h, 16)
                    return go
                PL.wait(s_a, s_a.n)
                PL.wait(d_ao, d_ao.n)
                PL.raw(mk_attn_out())
                d_ao.n += 16
                AC.wait(d_ao, d_ao.n)
    PL.wait(d_ao, d_ao.n)
    SP.wait(d_at, d_at.n)
    ar.release()
    if os.environ.get("KTAP") == "at":
        PL.wait(d_at, d_at.n)
        PL.dma_start(out=y.rearrange("a b -> (a b)")[0:8 * 128 * ROWS]
                     .rearrange("(p w) -> p w", p=128),
                     in_=at_d.ap().rearrange("v p t -> (v p t)")
                     .rearrange("(p w) -> p w", p=128)).then_inc(d_pl, 16)
        PL.wait(d_pl, d_pl.n)
    ckpt("p4b")
    pair_barrier()
    phase_barrier()

    ar.mark()
    wo_sb = ar.alloc("wo_sb", [128, 16, D], BF16)
    lsc1_16 = ar.alloc("lsc1_16", [128, NT_OWN], F32)
    lsc2_16 = ar.alloc("lsc2_16", [128, NT_OWN], F32)
    oacc = [ar.alloc(f"oacc{i}", [128, D], F32) for i in range(2)]
    pat = [ar.alloc(f"pat{i}", [128, 8, 128], BF16) for i in range(2)]
    mat = [ar.alloc(f"mat{i}", [128, 8, 128], BF16) for i in range(2)]
    d_pa = [mksem("d_pa0"), mksem("d_pa1")]
    SP.dma_start(out=wo_sb[:],
                 in_=wo_c.ap().rearrange("(c p) d -> p c d", p=128)
                 ).then_inc(d_kv, 16)
    WO_READY = d_kv.n
    for tch in range(NT_OWN):
        tb = tch % 2
        SP.wait(s_v, s_v.n)
        SP.dma_start(out=oacc[tb][:], in_=xb_t[tch]).then_inc(d_xp[tb], 16)

        def mk_pat(dst_i=tb, tch_=tch):
            def go(g_):
                rpar = PLregs["rpar"]
                for par in range(2):
                    src_par = 1 - par
                    ctx = (g_.If_eq(rpar, 0) if par == 0 else g_.Else())
                    with ctx:
                        g_.dma_start(
                            out=pat[dst_i][:],
                            in_=sh_attn[src_par, :, :,
                                        128 * tch_:128 * (tch_ + 1)]
                            .rearrange("c p t -> p c t")
                            ).then_inc(d_pa[dst_i].h, 16)
            return go
        PL.wait(d_pa[tb], d_pa[tb].n)
        PL.wait(s_pe, s_pe.n)
        PL.raw(mk_pat())
        d_pa[tb].n += 16
        SP.wait(s_pe, s_pe.n)
        SP.dma_start(out=mat[tb][:],
                     in_=at_d.ap()[:, :, 128 * tch:128 * (tch + 1)]
                     .rearrange("c p t -> p c t")).then_inc(d_q[tb], 16)
        if tch == 0:
            PE.wait(d_kv, WO_READY)
        PE.wait(d_pa[tb], d_pa[tb].n)
        PE.wait(d_q[tb], d_q[tb].n)
        PE.wait(s_v, s_v.n)
        for fc in range(16):
            lt = (mat[tb][:, fc, :] if fc < 8
                  else pat[tb][:, fc - 8, :])
            for q4 in range(4):
                PE.matmul(PS[q4][:], lt, wo_sb[:, fc, 512 * q4:512 * (q4 + 1)],
                          start=(fc == 0), stop=(fc == 15))
        PE.bump(s_pe, 1)
        VE.wait(s_pe, s_pe.n)
        VE.wait(d_xp[tb], d_xp[tb].n)
        for q4 in range(4):
            VE.tensor_tensor(out=oacc[tb][:, 512 * q4:512 * (q4 + 1)],
                             in0=PS[q4][:],
                             in1=oacc[tb][:, 512 * q4:512 * (q4 + 1)],
                             op=ALU.add)
        VE.drain()
        VE.reduce_sum(stats16[:, tch, 0:1], oacc[tb][:], axis=AX.X)
        VE.drain()
        VE.tensor_scalar(out=stats16[:, tch, 0:1], in0=stats16[:, tch, 0:1],
                         scalar1=1.0 / D, scalar2=None, op0=ALU.mult)
        VE.drain()
        VE.tensor_scalar(out=SCR[:], in0=oacc[tb][:],
                         scalar1=stats16[:, tch, 0:1],
                         scalar2=None, op0=ALU.subtract)
        VE.drain()
        VE.tensor_tensor(out=SCR[:], in0=SCR[:], in1=SCR[:], op=ALU.mult)
        VE.drain()
        VE.reduce_sum(stats16[:, tch, 1:2], SCR[:], axis=AX.X)
        VE.bump(s_v, 1)
        SP.wait(s_v, s_v.n)
        SP.dma_start(out=xb_t[tch], in_=oacc[tb][:]).then_inc(d_st, 16)
        VE.wait(d_st, d_st.n)
    VE.tensor_scalar(out=stats16[:, :, 1], in0=stats16[:, :, 1],
                     scalar1=1.0 / D, scalar2=1e-5,
                     op0=ALU.mult, op1=ALU.add)
    VE.drain()
    emit_rsqrt(VE, stats16[:, :, 1], stats16[:, :, 1], lsc1_16[:], lsc2_16[:])
    VE.bump(s_v, 1)
    ar.release()
    ar.release()
    ckpt("p4")

    y_t = y.rearrange("(n p) d -> n p d", p=128)
    emit_ffn("f2", y_t, False)
    ckpt("p5")

    for r in ALL:
        r.wait(d_st, d_st.n)
        r.wait(d_out, d_out.n)

    kt = os.environ.get("KTRUNC", "")
    if kt:
        snap = CKPT[kt]
        for r in ALL:
            del r.ops[snap[id(r)]:]
    with nc.Block() as block:
        block.sync(lambda e: SP.replay(e))
        block.gpsimd(lambda e: PL.replay(e))
        block.tensor(lambda e: PE.replay(e))
        block.vector(lambda e: VE.replay(e))
        block.scalar(lambda e: AC.replay(e))

    for g in sem_guards:
        g.__exit__(None, None, None)
    return nc


def _rope_tables(positions):
    inv = 1.0 / 10000.0 ** (np.arange(0, 64, 2)[:32].astype(np.float64) / 64)
    ang = np.outer(positions, inv)
    return np.cos(ang).astype(np.float32), np.sin(ang).astype(np.float32)


def make_in_maps(ii):
    x_full = ii["L0_attn_a_hidden_states"].reshape(NTOK, D)
    wkvmbT = np.ascontiguousarray(np.concatenate(
        [ii["L0_attn_a_Wk"].T, ii["L0_attn_a_Wv"].T,
         ii["L0_attn_a_Wmb"].T], axis=1))
    wqgT = np.ascontiguousarray(np.concatenate(
        [ii["L0_attn_a_Wq"].T, ii["L0_attn_a_Wg"].T], axis=1))
    bias_kvmb = np.concatenate(
        [ii["L0_attn_a_bk"], ii["L0_attn_a_bv"],
         ii["L0_attn_a_bmb"]])[None, :].astype(np.float32)
    bias_qg = np.concatenate(
        [ii["L0_attn_a_bq"], ii["L0_attn_a_bg"]])[None, :].astype(np.float32)
    mem = np.ascontiguousarray(ii["L0_attn_a_memory"].reshape(H * FD, 128))
    normT = np.ascontiguousarray(ii["L0_attn_a_norm"].T)
    def roll_cols(w, par):
        return w if par == 0 else np.concatenate(
            [w[:, 1024:], w[:, :1024]], axis=1)
    wq_b, wk_b, wv_b = (ii["L0_attn_b_wq"].T, ii["L0_attn_b_wk"].T,
                        ii["L0_attn_b_wv"].T)
    wo_b = ii["L0_attn_b_wout"].T
    wqkvbT_p, woT_p = [], []
    for par in range(2):
        wqkvbT_p.append(np.ascontiguousarray(np.concatenate(
            [roll_cols(wq_b, par), roll_cols(wk_b, par),
             roll_cols(wv_b, par)], axis=1)))
        woT_p.append(np.ascontiguousarray(
            wo_b if par == 0 else np.concatenate(
                [wo_b[1024:], wo_b[:1024]], axis=0)))
    lam = np.tile(np.concatenate(
        [ii["L0_attn_b_lambda_q1"], ii["L0_attn_b_lambda_q2"],
         ii["L0_attn_b_lambda_k1"], ii["L0_attn_b_lambda_k2"]]
    )[None, :].astype(np.float32), (128, 1))
    sublnw_r = np.ascontiguousarray(
        np.tile(ii["L0_attn_b_subln_weight"][None, :], (128, 1)))
    ident = np.eye(128, dtype=ml_dtypes.bfloat16)
    onesrow = np.ones((1, 128), dtype=ml_dtypes.bfloat16)

    in_maps = []
    for c in range(NCORES):
        par = c % 2
        pos = np.arange(2048 * par, 2048 * par + ROWS)
        cq, sq = _rope_tables(pos)
        in_maps.append({
            "xo": np.ascontiguousarray(x_full[ROWS * c:ROWS * (c + 1)]),
            "xk": np.ascontiguousarray(x_full[KROWS * par:KROWS * (par + 1)]),
            "wkvmbT": wkvmbT, "wqgT": wqgT,
            "bias_kvmb": bias_kvmb, "bias_qg": bias_qg,
            "mem": mem, "normT": normT,
            "wqkvbT": wqkvbT_p[par], "woT": woT_p[par], "lam": lam,
            "sublnw_r": sublnw_r,
            "cosq": np.ascontiguousarray(cq * QSCALE),
            "sinq": np.ascontiguousarray(sq * QSCALE),
            "cosk": cq, "sink": sq,
            "wg_f": np.ascontiguousarray(ii["L0_ffn_w_gate"]),
            "wu_f": np.ascontiguousarray(ii["L0_ffn_w_up"]),
            "wd_f": np.ascontiguousarray(ii["L0_ffn_w_down"]),
            "ident": ident, "onesrow": onesrow,
            "meta": np.array([[par, c, 0, 0, 0, 0, 0, 0]], np.int32),
        })

    return in_maps


def kernel(**inputs):
    ii = {k: np.asarray(v) for k, v in inputs.items()}
    if "nc" not in _CACHE:
        _CACHE["nc"] = build_program()
    nc = _CACHE["nc"]
    in_maps = make_in_maps(ii)
    res = run_bass_kernel_spmd(nc, in_maps, core_ids=list(range(NCORES)))
    out = np.concatenate([res.results[c]["y"] for c in range(NCORES)], axis=0)
    if DEBUG_TAPS:
        kernel.taps = {t: np.concatenate(
            [res.results[c]["tap_" + t] for c in range(NCORES)], axis=0)
            for t in DEBUG_TAPS}
    return out.reshape(4, SEQ, D)


# version 05 (final)
# speedup vs baseline: 1.9447x; 1.9447x over previous
import os
import numpy as np
import ml_dtypes
import concourse.bass as bass
import concourse.mybir as mybir
from concourse.bass_utils import run_bass_kernel_spmd

F32 = mybir.dt.float32
BF16 = mybir.dt.bfloat16
I32 = mybir.dt.int32
AF = mybir.ActivationFunctionType
ALU = mybir.AluOpType
AX = mybir.AxisListType

NCORES = 8
D = 2048
NTOK = 16384
ROWS = 2048
NT_OWN = 16
KROWS = 8192
NT_K = 64
H = 16
HKV = 4
FD = 768
FC = 6
I_FFN = 8192
SEQ = 4096
LAMBDA_INIT = 0.8 - 0.6 * float(np.exp(-0.3 * 12))
QSCALE = float(128 ** 0.5)
DELTA_W = 129 * H * FC

BUILD_PHASES = ("attn_a",)
DEBUG_TAPS = ()

_CACHE = {}


class Sem:
    def __init__(self, h):
        self.h = h
        self.n = 0


class _Tok:
    def __init__(self):
        self.chain = []

    def then_inc(self, sem, k=1):
        sem.n += k
        self.chain.append((sem, k))
        return self


class Rec:
    def __init__(self, name):
        self.name = name
        self.ops = []

    def __getattr__(self, attr):
        def rec(*a, **kw):
            tok = _Tok()
            self.ops.append((attr, a, kw, tok))
            return tok
        return rec

    def raw(self, fn):
        self.ops.append((None, (fn,), {}, None))

    def wait(self, sem, val=None):
        v = sem.n if val is None else val
        self.ops.append(("wait_ge", (sem.h, v), {}, None))

    def bump(self, sem, k=1):
        for i in range(len(self.ops) - 1, -1, -1):
            attr, a, kw, tok = self.ops[i]
            if attr not in (None, "wait_ge") and tok is not None:
                assert not tok.chain, (
                    "bump would stack a second update on one instruction; "
                    "restructure the sync at this site")
                tok.then_inc(sem, k)
                return
        t = _Tok()
        t.then_inc(sem, k)
        self.ops.append(("nop", (), {}, t))

    DRAIN_AFTER = {
        "tensor_tensor", "tensor_scalar", "tensor_single_scalar",
        "tensor_scalar_add", "tensor_scalar_sub", "tensor_scalar_mul",
        "tensor_scalar_max", "tensor_scalar_min", "tensor_copy",
        "tensor_relu", "tensor_reduce", "reduce_sum", "reduce_max",
        "scalar_tensor_tensor", "reciprocal", "memset", "activation",
        "copy", "square", "mul", "add", "sign", "sqrt",
    }

    def replay(self, eng):
        autodrain = self.name in ("vector", "scalar", "gpsimd")
        n = len(self.ops)
        for i, (attr, a, kw, tok) in enumerate(self.ops):
            if attr is None:
                a[0](eng)
                continue
            inst = getattr(eng, attr)(*a, **kw)
            if tok is not None:
                for (sem, k) in tok.chain:
                    inst = inst.then_inc(sem.h, k)
            if (autodrain and attr in self.DRAIN_AFTER
                    and not (i + 1 < n and self.ops[i + 1][0] == "drain")):
                eng.drain()


class Arena:
    def __init__(self, nc, base, top):
        self.nc = nc
        self.base = base
        self.top = top
        self.off = base
        self.marks = []

    def alloc(self, name, shape, dtype, align=64):
        per_part = int(np.prod(shape[1:])) * mybir.dt.size(dtype)
        self.off = (self.off + align - 1) // align * align
        t = self.nc.alloc_sbuf_tensor_at(name, list(shape), dtype,
                                         offset=self.off)
        self.off += per_part
        assert self.off <= self.top, (
            f"SBUF overflow at {name}: {self.off} > {self.top}")
        return t

    def mark(self):
        self.marks.append(self.off)

    def release(self):
        self.off = self.marks.pop()


def build_program():
    detect = os.environ.get("KRACE", "1") != "0"
    nc = bass.Bass("TRN2", target_bir_lowering=False,
                   detect_race_conditions=detect)

    def din(name, shape, dtype=F32):
        return nc.dram_tensor(name, list(shape), dtype, kind="ExternalInput").ap()

    xo = din("xo", [ROWS, D])
    xk = din("xk", [KROWS, D])
    wkvmbT = din("wkvmbT", [D, 3072])
    wqgT = din("wqgT", [D, 4096])
    bias_kvmb = din("bias_kvmb", [1, 3072])
    bias_qg = din("bias_qg", [1, 4096])
    mem_in = din("mem", [H * FD, 128])
    normT_in = din("normT", [FD, H])
    wqkvbT = din("wqkvbT", [D, 6144])
    woT = din("woT", [D, D])
    lam_in = din("lam", [128, 256])
    sublnw_r = din("sublnw_r", [128, 128])
    cosq = din("cosq", [ROWS, 32])
    sinq = din("sinq", [ROWS, 32])
    cosk = din("cosk", [ROWS, 32])
    sink = din("sink", [ROWS, 32])
    wg_f = din("wg_f", [D, I_FFN])
    wu_f = din("wu_f", [D, I_FFN])
    wd_f = din("wd_f", [I_FFN, D])
    ident_in = din("ident", [128, 128], BF16)
    ones_in = din("onesrow", [1, 128], BF16)
    meta_in = din("meta", [1, 8], I32)

    y = nc.dram_tensor("y", [ROWS, D], F32, kind="ExternalOutput").ap()
    taps = {}
    for t in DEBUG_TAPS:
        taps[t] = nc.dram_tensor("tap_" + t, [ROWS, D], F32,
                                 kind="ExternalOutput").ap()

    xbuf = nc.dram_tensor("xbuf", [ROWS, D], F32)
    wkvmb_c = nc.dram_tensor("wkvmb_c", [D, 3072], BF16)
    wqg_c = nc.dram_tensor("wqg_c", [D, 4096], BF16)
    wqkvb_c = nc.dram_tensor("wqkvb_c", [D, 6144], BF16)
    wo_c = nc.dram_tensor("wo_c", [D, D], BF16)
    wgf_c = nc.dram_tensor("wgf_c", [D, I_FFN], BF16)
    wuf_c = nc.dram_tensor("wuf_c", [D, I_FFN], BF16)
    wdf_c = nc.dram_tensor("wdf_c", [I_FFN, D], BF16)

    sh_delta = nc.dram_tensor("sh_delta", [2, 128, DELTA_W], F32,
                              addr_space="Shared")
    sh_qkv = nc.dram_tensor("sh_qkv", [2, 3, 1024, ROWS], BF16,
                            addr_space="Shared")
    sh_attn = nc.dram_tensor("sh_attn", [2, 8, 128, ROWS], BF16,
                             addr_space="Shared")
    qT_d = nc.dram_tensor("qT_d", [1024, ROWS], BF16)
    kT_d = nc.dram_tensor("kT_d", [1024, ROWS], BF16)
    v_d = nc.dram_tensor("v_d", [ROWS, 1024], BF16)
    at_d = nc.dram_tensor("at_d", [8, 128, ROWS], BF16)
    md_d = nc.dram_tensor("md_d", [16, SEQ], F32)
    sh_flag = nc.dram_tensor("sh_flag", [2, 16], I32, addr_space="Shared")

    PS = [nc.alloc_psum_tensor(f"ps{i}", [128, 512], F32) for i in range(8)]
    PSb = [p[:].bitcast(BF16) for p in PS]

    SP, PL, PE, VE, AC = (Rec("sync"), Rec("gpsimd"), Rec("tensor"),
                          Rec("vector"), Rec("scalar"))
    sem_guards = []

    def mksem(name):
        g = nc.semaphore(name)
        h = g.__enter__()
        sem_guards.append(g)
        return Sem(h)

    d_x = mksem("d_x")
    d_xp = [mksem("d_x0"), mksem("d_x1")]
    d_w = mksem("d_w")
    d_wp = [mksem("d_w0"), mksem("d_w1")]
    d_st = mksem("d_st")
    d_pl = mksem("d_pl")
    d_acc = mksem("d_acc")
    d_out = mksem("d_out")
    s_pe = mksem("s_pe")
    s_v = mksem("s_v")
    s_a = mksem("s_a")
    s_pl = mksem("s_pl")
    pd = mksem("pd")
    phz = mksem("phz")

    ALL = (SP, PL, PE, VE, AC)

    CKPT = {}

    def ckpt(name):
        CKPT[name] = {id(r): len(r.ops) for r in ALL}

    def phase_barrier():
        for dsem in (d_x, d_xp[0], d_xp[1], d_w, d_wp[0], d_wp[1], d_st,
                     d_out):
            SP.wait(dsem, dsem.n)
        SP.nop().then_inc(phz, 1)
        for dsem in (d_pl, d_acc):
            PL.wait(dsem, dsem.n)
        PL.nop().then_inc(phz, 1)
        for r in (PE, VE, AC):
            r.drain()
            r.bump(phz, 1)
        tgt = phz.n
        for r in ALL:
            r.wait(phz, tgt)

    bar = {"k": 0}

    def pair_barrier(extra=None):
        bar["k"] += 1
        K = bar["k"]
        if os.environ.get("KSOLO") == "1":
            return
        psem = mksem(f"pb{K}")

        def body(g):
            rpar = g.alloc_register(f"b{K}_rpar")
            rflag = g.alloc_register(f"b{K}_rflag")
            rcond = g.alloc_register(f"b{K}_rcond")
            rpd = g.alloc_register(f"b{K}_rpd")
            g.reg_load(rpar, meta[0:1, 0:1])
            g.reg_mov(rpd, 0)

            def dma(dst, src):
                g.reg_add(rpd, rpd, 16)
                g.dma_start(out=dst, in_=src).then_inc(psem.h, 16)
                g.wait_ge(psem.h, rpd)

            g.memset(flagk[:], K)
            g.drain()
            with g.If_eq(rpar, 0):
                dma(sh_flag[0, :][None, :], flagk[:])
            with g.Else():
                dma(sh_flag[1, :][None, :], flagk[:])
            g.reg_mov(rflag, 0)

            def cond():
                g.reg_alu(rcond, rflag, K, ALU.is_lt)
                return rcond
            with g.While(cond):
                with g.If_eq(rpar, 0):
                    dma(flagp[:], sh_flag[1, :][None, :])
                with g.Else():
                    dma(flagp[:], sh_flag[0, :][None, :])
                g.reg_load(rflag, flagp[0:1, 0:1])
            if extra is not None:
                extra(g, dma)
            for r in (rpar, rflag, rcond, rpd):
                g.free_register(r)
        PL.raw(body)

    def emit_rsqrt(rec, out_ap, in_ap, scr_a, scr_y, floor=1e-30, n_iter=3):
        rec.tensor_scalar_max(scr_a, in_ap, float(floor))
        rec.drain()
        bi = scr_y.bitcast(I32)
        rec.tensor_scalar(out=bi, in0=scr_a.bitcast(I32), scalar1=1,
                          scalar2=None, op0=ALU.arith_shift_right)
        rec.drain()
        rec.tensor_scalar(out=bi, in0=bi, scalar1=-1, scalar2=0x5f3759df,
                          op0=ALU.mult, op1=ALU.add)
        rec.drain()
        for _ in range(n_iter):
            rec.tensor_tensor(out=out_ap, in0=scr_y, in1=scr_y, op=ALU.mult)
            rec.drain()
            rec.tensor_tensor(out=out_ap, in0=out_ap, in1=scr_a, op=ALU.mult)
            rec.drain()
            rec.tensor_scalar(out=out_ap, in0=out_ap, scalar1=-0.5,
                              scalar2=1.5, op0=ALU.mult, op1=ALU.add)
            rec.drain()
            rec.tensor_tensor(out=scr_y, in0=scr_y, in1=out_ap, op=ALU.mult)
            rec.drain()
        rec.tensor_copy(out=out_ap, in_=scr_y)
        rec.drain()

    ar = Arena(nc, nc.sbuf_base, nc.sbuf_top)
    ident = ar.alloc("identsb", [128, 128], BF16)
    onesr = ar.alloc("onesr", [1, 128], BF16)
    meta = ar.alloc("metasb", [1, 8], I32)
    flagk = ar.alloc("flagk", [1, 16], I32)
    flagp = ar.alloc("flagp", [1, 16], I32)
    stats16 = ar.alloc("stats16", [128, NT_OWN, 2], F32)
    SCR = ar.alloc("SCR", [128, D], F32)
    ar.mark()
    mem_aug = ar.alloc("mem_aug", [128, H * FC, 128], BF16)
    nrmcol = ar.alloc("nrmcol", [128, FC, H], BF16)
    nrm32 = ar.alloc("nrm32", [128, FC, H], F32)
    brow_kvmb = ar.alloc("brow_kvmb", [1, 3072], BF16)
    brow_qg = ar.alloc("brow_qg", [1, 4096], BF16)

    SP.dma_start(out=ident[:], in_=ident_in).then_inc(d_x, 16)
    SP.dma_start(out=onesr[:], in_=ones_in).then_inc(d_x, 16)
    SP.dma_start(out=meta[:], in_=meta_in).then_inc(d_x, 16)
    CONSTS_T = d_x.n

    PLregs = {}

    def _init_regs(g):
        g.wait_ge(d_x.h, CONSTS_T)
        PLregs["rpar"] = g.alloc_register("rpar_glob")
        g.reg_load(PLregs["rpar"], meta[0:1, 0:1])
    PL.raw(_init_regs)

    mem_r = mem_in.rearrange("(g p) d -> p g d", p=128)
    normT_r = normT_in.rearrange("(fc p) h -> p fc h", p=128)
    PL.dma_start(out=mem_aug[:], in_=mem_r).then_inc(d_pl, 16)
    PL.wait(d_pl, d_pl.n)
    PL.dma_start(out=nrm32[:], in_=normT_r).then_inc(d_pl, 16)
    PL.wait(d_pl, d_pl.n)
    PL.dma_start(out=nrmcol[:], in_=normT_r).then_inc(d_pl, 16)
    PL.wait(d_pl, d_pl.n)
    PL.dma_start(out=brow_kvmb[:], in_=bias_kvmb).then_inc(d_pl, 16)
    PL.wait(d_pl, d_pl.n)
    PL.dma_start(out=brow_qg[:], in_=bias_qg).then_inc(d_pl, 16)
    PL.wait(d_pl, d_pl.n)
    P0_POOL_T = d_pl.n

    CACHE_T = {}
    for dst, src in ((wqkvb_c, wqkvbT), (wo_c, woT)):
        PL.dma_start(out=dst.ap(), in_=src).then_inc(d_pl, 16)
        PL.wait(d_pl, d_pl.n)
        CACHE_T[dst.name] = [(d_pl, d_pl.n)]
    ar.mark()
    sf32 = [ar.alloc(f"cst32_{i}", [128, 8192], F32) for i in range(2)]
    sbf = [ar.alloc(f"cstbf_{i}", [128, 8192], BF16) for i in range(2)]
    a_after = [None, None]
    cb = 0
    for dst, src, R, C in ((wkvmb_c, wkvmbT, D, 3072),
                           (wqg_c, wqgT, D, 4096),
                           (wgf_c, wg_f, D, I_FFN),
                           (wuf_c, wu_f, D, I_FFN),
                           (wdf_c, wd_f, I_FFN, D)):
        src_r = src.rearrange("(n p) c -> n p c", p=128)
        dst_r = dst.ap().rearrange("(n p) c -> n p c", p=128)
        for n in range(R // 128):
            pi = cb % 2
            if a_after[pi] is not None:
                SP.wait(s_a, a_after[pi])
            SP.dma_start(out=sf32[pi][:, 0:C], in_=src_r[n]
                         ).then_inc(d_xp[pi], 16)
            AC.wait(d_xp[pi], d_xp[pi].n)
            AC.wait(d_wp[pi], d_wp[pi].n)
            AC.copy(out=sbf[pi][:, 0:C], in_=sf32[pi][:, 0:C])
            AC.bump(s_a, 1)
            a_after[pi] = s_a.n
            AC.dma_start(out=dst_r[n], in_=sbf[pi][:, 0:C]
                         ).then_inc(d_wp[pi], 16)
            cb += 1
        CACHE_T[dst.name] = [(d_wp[0], d_wp[0].n), (d_wp[1], d_wp[1].n)]
    for r in (SP, VE, PL):
        r.wait(d_wp[0], d_wp[0].n)
        r.wait(d_wp[1], d_wp[1].n)
    ar.release()
    ckpt("p0")

    ar.mark()
    kstat_m = ar.alloc("kstat_m", [128, NT_K], F32)
    kstat_r = ar.alloc("kstat_r", [128, NT_K], F32)
    kscr1 = ar.alloc("kscr1", [128, NT_K], F32)
    kscr2 = ar.alloc("kscr2", [128, NT_K], F32)
    xt2 = [ar.alloc(f"xt{i}", [128, D], F32) for i in range(2)]
    hsb = ar.alloc("hsb", [128, D], BF16)
    hsT_g = [ar.alloc(f"hsTg{i}", [128, 16, 128], BF16) for i in range(4)]
    wbuf = [ar.alloc(f"wbuf{i}", [128, 16, 256], BF16) for i in range(2)]
    kst = [ar.alloc(f"kst{i}", [128, 512], F32) for i in range(4)]
    vst = [ar.alloc(f"vst{i}", [128, 512], F32) for i in range(4)]
    mbst = [ar.alloc(f"mbst{i}", [128, D], BF16) for i in range(4)]
    xr = ar.alloc("xr", [128, HKV, 256], F32)
    dpk32 = ar.alloc("dpk32", [128, HKV, FD], F32)
    dpkb = [ar.alloc(f"dpkb{i}", [128, HKV * FD], BF16) for i in range(4)]
    dpkT = ar.alloc("dpkT", [128, HKV * FC, 128], BF16)
    wmv = [ar.alloc(f"wmv{i}", [128, H, 129], BF16) for i in range(4)]
    n2 = ar.alloc("n2", [128, HKV], F32)
    rsi = ar.alloc("rsi", [128, HKV], F32)
    sca = ar.alloc("sca", [128, HKV], F32)
    scb = ar.alloc("scb", [128, HKV], F32)
    s1e8 = ar.alloc("s1e8", [128, HKV], F32)
    dDp = ar.alloc("dDp", [128, H], F32)
    recD = ar.alloc("recD", [128, H], F32)
    negD = ar.alloc("negD", [128, H], F32)
    dcoef = ar.alloc("dcoef", [128, H], F32)
    t128 = ar.alloc("t128", [128, 128], F32)
    stg = ar.alloc("stg", [128, 8, 387], F32)

    xk_t = xk.rearrange("(n p) d -> n p d", p=128)

    T_x = []
    SP.dma_start(out=xt2[0][:], in_=xk_t[0]).then_inc(d_xp[0], 16)
    T_x.append((0, d_xp[0].n))
    for T in range(NT_K):
        if T + 1 < NT_K:
            if T >= 1:
                SP.wait(s_v, s_v.n)
            SP.dma_start(out=xt2[(T + 1) % 2][:], in_=xk_t[T + 1]
                         ).then_inc(d_xp[(T + 1) % 2], 16)
            T_x.append(((T + 1) % 2, d_xp[(T + 1) % 2].n))
        xt = xt2[T % 2][:]
        VE.wait(d_xp[T_x[T][0]], T_x[T][1])
        VE.reduce_sum(kstat_m[:, T:T + 1], xt, axis=AX.X)
        VE.drain()
        VE.tensor_scalar(out=kstat_m[:, T:T + 1], in0=kstat_m[:, T:T + 1],
                         scalar1=1.0 / D, scalar2=None, op0=ALU.mult)
        VE.drain()
        VE.tensor_scalar(out=SCR[:], in0=xt, scalar1=kstat_m[:, T:T + 1],
                         scalar2=None, op0=ALU.subtract)
        VE.drain()
        VE.tensor_tensor(out=SCR[:], in0=SCR[:], in1=SCR[:], op=ALU.mult)
        VE.drain()
        VE.reduce_sum(kstat_r[:, T:T + 1], SCR[:], axis=AX.X)
        VE.bump(s_v, 1)
    VE.drain()
    VE.tensor_scalar(out=kstat_r[:], in0=kstat_r[:], scalar1=1.0 / D,
                     scalar2=1e-5, op0=ALU.mult, op1=ALU.add)
    VE.drain()
    emit_rsqrt(VE, kstat_r[:], kstat_r[:], kscr1[:], kscr2[:])
    VE.bump(s_v, 1)
    ckpt("p1a")

    PE.wait(d_x, CONSTS_T)
    PE.wait(d_pl, P0_POOL_T)
    for _sm, _vv in CACHE_T["wkvmb_c"]:
        PE.wait(_sm, _vv)

    v_sv_hsT = {}
    for G in range(16):
        VE.wait(s_a, s_a.n)
        for t in range(4):
            T = 4 * G + t
            xt = xt2[T % 2][:]
            SP.wait(s_v, s_v.n)
            SP.dma_start(out=xt2[T % 2][:], in_=xk_t[T]
                         ).then_inc(d_xp[T % 2], 16)
            VE.wait(d_xp[T % 2], d_xp[T % 2].n)
            VE.tensor_scalar(out=SCR[:], in0=xt,
                             scalar1=kstat_m[:, T:T + 1],
                             scalar2=kstat_r[:, T:T + 1],
                             op0=ALU.subtract, op1=ALU.mult)
            VE.tensor_copy(out=hsb[:], in_=SCR[:])
            VE.bump(s_v, 1)
            if G == 0 and t == 0:
                ckpt("m_ln")
            PE.wait(s_v, s_v.n)
            for half in range(2):
                for j in range(8):
                    c = half * 8 + j
                    PE.transpose(PSb[6][:, 128 * j:128 * (j + 1)],
                                 hsb[:, 128 * c:128 * (c + 1)], ident[:])
                PE.bump(s_pe, 1)
                if G == 0 and t == 0 and half == 0:
                    ckpt("m_tr")
                AC.wait(s_pe, s_pe.n)
                AC.copy(out=hsT_g[t][:, 8 * half:8 * half + 8, :]
                        .rearrange("p a b -> p (a b)"), in_=PSb[6][:, 0:1024])
                AC.bump(s_a, 1)
                if G == 0 and t == 0 and half == 0:
                    ckpt("m_cp")
                if G == 0 and t == 0 and half == 1:
                    ckpt("m_cp2")
                PE.wait(s_a, s_a.n)
            VE.wait(s_a, s_a.n)
            if G == 0:
                ckpt(f"p1fa{t}")

        def ob_issue(ob_):
            SP.wait(s_pe, s_pe.n)
            SP.dma_start(
                out=wbuf[ob_ % 2][:],
                in_=wkvmb_c.ap()[:, 256 * ob_:256 * (ob_ + 1)]
                .rearrange("(dc p) o -> p dc o", p=128)
                ).then_inc(d_wp[ob_ % 2], 16)
            return d_wp[ob_ % 2].n
        ob_pend = {0: ob_issue(0)}
        for ob in range(12):
            buf = wbuf[ob % 2]
            if ob + 1 < 12:
                ob_pend[ob + 1] = ob_issue(ob + 1)
            PE.wait(d_wp[ob % 2], ob_pend[ob])
            for t in range(4):
                psu = PS[2 + t][:, 256 * (ob % 2):256 * (ob % 2 + 1)]
                for dc in range(16):
                    PE.matmul(psu, hsT_g[t][:, dc, :], buf[:, dc, :],
                              start=(dc == 0), stop=False)
                PE.matmul(psu, onesr[:],
                          brow_kvmb[:, 256 * ob:256 * (ob + 1)],
                          start=False, stop=True)
            PE.bump(s_pe, 1)
            if ob % 2 == 1:
                for t in range(4):
                    seg = ob // 2
                    if seg == 0:
                        VE.wait(s_pe, s_pe.n)
                        VE.tensor_copy(out=kst[t][:], in_=PS[2 + t][:])
                        VE.bump(s_v, 1)
                    elif seg == 1:
                        VE.wait(s_pe, s_pe.n)
                        VE.tensor_copy(out=vst[t][:], in_=PS[2 + t][:])
                        VE.bump(s_v, 1)
                    else:
                        AC.wait(s_pe, s_pe.n)
                        AC.activation(out=mbst[t][:, 512 * (seg - 2):
                                                  512 * (seg - 1)],
                                      in_=PS[2 + t][:], func=AF.Sigmoid)
                        AC.bump(s_a, 1)
                PE.wait(s_v, s_v.n)
                PE.wait(s_a, s_a.n)
            if G == 0:
                ckpt(f"p1ob{ob}")

        for t in range(4):
            T = 4 * G + t
            kv = kst[t][:].rearrange("p (h d) -> p h d", h=HKV)
            VE.tensor_relu(xr[:, :, 0:128], kv)
            VE.tensor_scalar(out=xr[:, :, 128:256], in0=kv, scalar1=-1.0,
                             scalar2=0.0, op0=ALU.mult, op1=ALU.max)
            VE.drain()
            for j in (1, 2, 3):
                o = 256 * (j - 1)
                VE.tensor_tensor(out=dpk32[:, :, o + j:o + 256],
                                 in0=xr[:, :, j:256],
                                 in1=xr[:, :, 0:256 - j], op=ALU.mult)
                VE.tensor_tensor(out=dpk32[:, :, o:o + j],
                                 in0=xr[:, :, 0:j],
                                 in1=xr[:, :, 256 - j:256], op=ALU.mult)
            VE.drain()
            VE.tensor_copy(out=dpkb[t][:], in_=dpk32[:]
                           .rearrange("p a b -> p (a b)"))
            VE.bump(s_v, 1)
            AC.wait(s_v, s_v.n)
            for hh in range(HKV):
                AC.activation(out=SCR[:, 0:FD], in_=dpk32[:, hh, :],
                              func=AF.Square,
                              accum_out=n2[:, hh:hh + 1])
            AC.bump(s_a, 1)
            VE.wait(s_a, s_a.n)
            emit_rsqrt(VE, rsi[:], n2[:], sca[:], scb[:], floor=1e-24)
            VE.tensor_tensor(out=s1e8[:], in0=n2[:], in1=rsi[:], op=ALU.mult)
            VE.drain()
            VE.tensor_scalar(out=s1e8[:], in0=s1e8[:], scalar1=1e-8,
                             scalar2=None, op0=ALU.mult)
            VE.drain()

            PE.wait(s_v, s_v.n)
            for rnd in range(3):
                for j in range(8):
                    c = rnd * 8 + j
                    PE.transpose(PSb[6][:, 128 * j:128 * (j + 1)],
                                 dpkb[t][:, 128 * c:128 * (c + 1)], ident[:])
                PE.bump(s_pe, 1)
                AC.wait(s_pe, s_pe.n)
                AC.copy(out=dpkT[:, 8 * rnd:8 * rnd + 8, :]
                        .rearrange("p a b -> p (a b)"), in_=PSb[6][:, 0:1024])
                AC.bump(s_a, 1)
                PE.wait(s_a, s_a.n)

            if t > 0 or G > 0:
                PE.wait(s_v, s_v.n)
            for g in range(HKV):
                pnum = PS[g]
                for fc in range(FC):
                    lt = dpkT[:, g * FC + fc, :]
                    rhs = mem_aug[:, (4 * g) * FC + fc:
                                  (4 * g + 3) * FC + fc + 1:FC, :]
                    PE.matmul(pnum[:], lt, rhs,
                              start=(fc == 0), stop=(fc == FC - 1))
                    PE.matmul(PS[7][:, 4 * g:4 * g + 4], lt,
                              nrmcol[:, fc, 4 * g:4 * g + 4],
                              start=(fc == 0), stop=(fc == FC - 1))
            PE.bump(s_pe, 1)

            VE.wait(s_pe, s_pe.n)
            VE.wait(s_a, s_a.n)
            for g in range(HKV):
                VE.tensor_scalar(out=dDp[:, 4 * g:4 * g + 4],
                                 in0=PS[7][:, 4 * g:4 * g + 4],
                                 scalar1=s1e8[:, g:g + 1], scalar2=None,
                                 op0=ALU.add)
            VE.drain()
            VE.reciprocal(recD[:], dDp[:])
            VE.drain()
            VE.tensor_scalar(out=negD[:], in0=recD[:],
                             scalar1=-1.0, scalar2=None, op0=ALU.mult)
            for g in range(HKV):
                VE.tensor_scalar(out=dcoef[:, 4 * g:4 * g + 4],
                                 in0=dDp[:, 4 * g:4 * g + 4],
                                 scalar1=rsi[:, g:g + 1], scalar2=-1.0,
                                 op0=ALU.mult, op1=ALU.mult)
            VE.drain()
            VE.tensor_scalar(out=dcoef[:], in0=dcoef[:],
                             scalar1=1.0, scalar2=0.0,
                             op0=ALU.add, op1=ALU.max)
            VE.drain()
            VE.tensor_scalar(out=dcoef[:], in0=dcoef[:],
                             scalar1=1.0, scalar2=None, op0=ALU.min)
            VE.drain()
            for g in range(HKV):
                VE.tensor_scalar(out=dcoef[:, 4 * g:4 * g + 4],
                                 in0=dcoef[:, 4 * g:4 * g + 4],
                                 scalar1=rsi[:, g:g + 1], scalar2=None,
                                 op0=ALU.mult)
            VE.drain()
            for g in range(HKV):
                for j in range(4):
                    h = 4 * g + j
                    VE.scalar_tensor_tensor(
                        out=t128[:], in0=PS[g][:, 128 * j:128 * (j + 1)],
                        scalar=negD[:, h:h + 1],
                        in1=vst[t][:, 128 * g:128 * (g + 1)],
                        op0=ALU.mult, op1=ALU.add)
                    VE.drain()
                    VE.scalar_tensor_tensor(
                        out=wmv[t][:, h, 0:128], in0=t128[:],
                        scalar=rsi[:, g:g + 1],
                        in1=mbst[t][:, 128 * h:128 * (h + 1)],
                        op0=ALU.mult, op1=ALU.mult)
                    VE.drain()
            VE.bump(s_v, 1)
            VE.tensor_copy(out=wmv[t][:, :, 128:129],
                           in_=dcoef[:].rearrange("p (h o) -> p h o", o=1))
            VE.drain()
            VE.bump(s_v, 1)
            v_sv_hsT[("wmv", G, t)] = s_v.n
            if G == 0:
                ckpt(f"p1fb{t}")

        PE.wait(s_v, v_sv_hsT[("wmv", G, 3)])
        for rnd in range(4):
            for sl in range(24):
                j = 24 * rnd + sl
                h, fc = j // FC, j % FC
                g = h // 4
                bank, slot = sl // 3, sl % 3
                out = PS[bank][:, 129 * slot:129 * (slot + 1)]
                for t in range(4):
                    PE.matmul(out,
                              dpkb[t][:, (g * FC + fc) * 128:
                                      (g * FC + fc + 1) * 128],
                              wmv[t][:, h, :],
                              start=(t == 0), stop=(t == 3))
            PE.bump(s_pe, 1)
            VE.wait(d_acc, d_acc.n)
            VE.wait(s_pe, s_pe.n)
            for bank in range(8):
                VE.tensor_copy(out=stg[:, bank, :], in_=PS[bank][:, 0:387])
            VE.bump(s_v, 1)
            PL.wait(s_v, s_v.n)
            base = (24 * rnd) * 129
            acc_op = (ALU.bypass if G == 0 else ALU.add)

            def mkd(b=base, a=acc_op):
                def go(g_):
                    rpar = PLregs["rpar"]
                    src = stg[:].rearrange("p a b -> p (a b)")
                    with g_.If_eq(rpar, 0):
                        g_.dma_start(out=sh_delta[0, :, b:b + 3096],
                                     in_=src, accum_op=a,
                                     max_dma_last_dim=2048
                                     ).then_inc(d_acc.h, 16)
                    with g_.Else():
                        g_.dma_start(out=sh_delta[1, :, b:b + 3096],
                                     in_=src, accum_op=a,
                                     max_dma_last_dim=2048
                                     ).then_inc(d_acc.h, 16)
                return go
            PL.raw(mkd())
            d_acc.n += 16
            PE.wait(s_v, s_v.n)
        if G == 0:
            ckpt("p1g0")

    PL.wait(d_acc, d_acc.n)
    ckpt("p1")
    pair_barrier()
    ckpt("p1bar")

    ar.release()
    ar.mark()
    bA = ar.alloc("bA", [128, 2064], F32)
    bB = ar.alloc("bB", [128, 2064], F32)
    bC = ar.alloc("bC", [128, 16, 128], F32)
    phase_barrier()
    for c in range(6):
        base = c * 16 * 129
        if os.environ.get("KSOLO") == "1":
            VE.memset(bA[:], 0.0)
            VE.memset(bB[:], 0.0)
            VE.drain()
            VE.bump(s_v, 1)
            PL.wait(s_v, s_v.n)
        else:
            PL.dma_start(out=bA[:], in_=sh_delta[0, :, base:base + 2064]
                         ).then_inc(d_pl, 16)
            PL.wait(d_pl, d_pl.n)
            PL.dma_start(out=bB[:], in_=sh_delta[1, :, base:base + 2064]
                         ).then_inc(d_pl, 16)
            PL.wait(d_pl, d_pl.n)
        PL.dma_start(out=bC[:], in_=mem_r[:, 16 * c:16 * (c + 1), :]
                     ).then_inc(d_pl, 16)
        PL.wait(d_pl, d_pl.n)
        VE.wait(d_pl, d_pl.n)
        VE.tensor_tensor(out=bA[:], in0=bA[:], in1=bB[:], op=ALU.add)
        VE.drain()
        bAv = bA[:].rearrange("p (s w) -> p s w", w=129)
        VE.tensor_tensor(out=mem_aug[:, 16 * c:16 * (c + 1), :],
                         in0=bAv[:, :, 0:128], in1=bC[:], op=ALU.add)
        VE.drain()
        for s in range(16):
            j = 16 * c + s
            h, fc = j // FC, j % FC
            VE.tensor_tensor(out=nrm32[:, fc, h:h + 1],
                             in0=nrm32[:, fc, h:h + 1],
                             in1=bAv[:, s, 128:129], op=ALU.add)
        VE.drain()
        VE.bump(s_v, 1)
        PL.wait(s_v, s_v.n)
    VE.tensor_copy(out=nrmcol[:], in_=nrm32[:])
    VE.drain()
    VE.bump(s_v, 1)
    if os.environ.get("KTAP") == "nm":
        PL.wait(s_v, s_v.n)
        yflat = y.rearrange("a b -> (a b)")
        PL.dma_start(out=yflat[0:128 * 12288].rearrange("(p w) -> p w", p=128),
                     in_=mem_aug[:].rearrange("p a b -> p (a b)")
                     ).then_inc(d_pl, 16)
        PL.wait(d_pl, d_pl.n)
        PL.dma_start(out=yflat[128 * 12288:128 * 12288 + 128 * 96]
                     .rearrange("(p w) -> p w", p=128),
                     in_=nrm32[:].rearrange("p a b -> p (a b)")
                     ).then_inc(d_pl, 16)
        PL.wait(d_pl, d_pl.n)
    ckpt("p15")
    ar.release()

    ar.mark()
    ostat_m = ar.alloc("ostat_m", [128, NT_OWN], F32)
    ostat_r = ar.alloc("ostat_r", [128, NT_OWN], F32)
    oscr1 = ar.alloc("oscr1", [128, NT_OWN], F32)
    oscr2 = ar.alloc("oscr2", [128, NT_OWN], F32)
    xt2b = [ar.alloc(f"xtb{i}", [128, D], F32) for i in range(2)]
    hsb2 = ar.alloc("hsb2", [128, D], BF16)
    hsT2 = [ar.alloc(f"hsT2{i}", [128, 16, 128], BF16) for i in range(2)]
    wbuf2 = [ar.alloc(f"wbuf2{i}", [128, 16, 256], BF16) for i in range(2)]
    qst = [ar.alloc(f"qst{i}", [128, D], F32) for i in range(2)]
    gatest = [ar.alloc(f"gatest{i}", [128, D], F32) for i in range(2)]
    xr2 = ar.alloc("xr2", [128, 8, 256], F32)
    dpq32 = ar.alloc("dpq32", [128, 8, FD], F32)
    dpqb = ar.alloc("dpqb", [128, 8 * FD], BF16)
    dpqT = ar.alloc("dpqT", [128, 8 * FC, 128], BF16)
    n2q = ar.alloc("n2q", [128, 16], F32)
    rsiq = ar.alloc("rsiq", [128, 16], F32)
    scaq = ar.alloc("scaq", [128, 16], F32)
    scbq = ar.alloc("scbq", [128, 16], F32)
    s1e8q = ar.alloc("s1e8q", [128, 16], F32)
    dDq = ar.alloc("dDq", [128, 16], F32)
    recDq = ar.alloc("recDq", [128, 16], F32)
    mh = ar.alloc("mh", [128, D], F32)

    xo_t = xo.rearrange("(n p) d -> n p d", p=128)
    xb_t = xbuf.ap().rearrange("(n p) d -> n p d", p=128)

    phase_barrier()
    for _sm, _vv in CACHE_T["wqg_c"]:
        PE.wait(_sm, _vv)

    T_x2 = []
    SP.dma_start(out=xt2b[0][:], in_=xo_t[0]).then_inc(d_xp[0], 16)
    T_x2.append((0, d_xp[0].n))
    for T in range(NT_OWN):
        if T + 1 < NT_OWN:
            if T >= 1:
                SP.wait(s_v, s_v.n)
            SP.dma_start(out=xt2b[(T + 1) % 2][:], in_=xo_t[T + 1]
                         ).then_inc(d_xp[(T + 1) % 2], 16)
            T_x2.append(((T + 1) % 2, d_xp[(T + 1) % 2].n))
        xt = xt2b[T % 2][:]
        VE.wait(d_xp[T_x2[T][0]], T_x2[T][1])
        VE.reduce_sum(ostat_m[:, T:T + 1], xt, axis=AX.X)
        VE.drain()
        VE.tensor_scalar(out=ostat_m[:, T:T + 1], in0=ostat_m[:, T:T + 1],
                         scalar1=1.0 / D, scalar2=None, op0=ALU.mult)
        VE.drain()
        VE.tensor_scalar(out=SCR[:], in0=xt, scalar1=ostat_m[:, T:T + 1],
                         scalar2=None, op0=ALU.subtract)
        VE.drain()
        VE.tensor_tensor(out=SCR[:], in0=SCR[:], in1=SCR[:], op=ALU.mult)
        VE.drain()
        VE.reduce_sum(ostat_r[:, T:T + 1], SCR[:], axis=AX.X)
        VE.bump(s_v, 1)
    VE.drain()
    VE.tensor_scalar(out=ostat_r[:], in0=ostat_r[:], scalar1=1.0 / D,
                     scalar2=1e-5, op0=ALU.mult, op1=ALU.add)
    VE.drain()
    emit_rsqrt(VE, ostat_r[:], ostat_r[:], oscr1[:], oscr2[:])
    VE.bump(s_v, 1)

    for G in range(8):
        VE.wait(s_a, s_a.n)
        for t in range(2):
            T = 2 * G + t
            xt = xt2b[T % 2][:]
            SP.wait(s_v, s_v.n)
            SP.dma_start(out=xt2b[T % 2][:], in_=xo_t[T]
                         ).then_inc(d_xp[T % 2], 16)
            VE.wait(d_xp[T % 2], d_xp[T % 2].n)
            VE.tensor_scalar(out=SCR[:], in0=xt,
                             scalar1=ostat_m[:, T:T + 1],
                             scalar2=ostat_r[:, T:T + 1],
                             op0=ALU.subtract, op1=ALU.mult)
            VE.tensor_copy(out=hsb2[:], in_=SCR[:])
            VE.bump(s_v, 1)
            PE.wait(s_v, s_v.n)
            for half in range(2):
                for j in range(8):
                    c = half * 8 + j
                    PE.transpose(PSb[6][:, 128 * j:128 * (j + 1)],
                                 hsb2[:, 128 * c:128 * (c + 1)], ident[:])
                PE.bump(s_pe, 1)
                AC.wait(s_pe, s_pe.n)
                AC.copy(out=hsT2[t][:, 8 * half:8 * half + 8, :]
                        .rearrange("p a b -> p (a b)"), in_=PSb[6][:, 0:1024])
                AC.bump(s_a, 1)
                PE.wait(s_a, s_a.n)
            VE.wait(s_a, s_a.n)

        def qg_issue(ob_):
            SP.wait(s_pe, s_pe.n)
            SP.dma_start(
                out=wbuf2[ob_ % 2][:],
                in_=wqg_c.ap()[:, 256 * ob_:256 * (ob_ + 1)]
                .rearrange("(dc p) o -> p dc o", p=128)
                ).then_inc(d_wp[ob_ % 2], 16)
            return d_wp[ob_ % 2].n
        qg_pend = {0: qg_issue(0)}
        for ob in range(16):
            buf = wbuf2[ob % 2]
            if ob + 1 < 16:
                qg_pend[ob + 1] = qg_issue(ob + 1)
            PE.wait(d_wp[ob % 2], qg_pend[ob])
            for t in range(2):
                psu = PS[2 + t][:, 256 * (ob % 2):256 * (ob % 2 + 1)]
                for dc in range(16):
                    PE.matmul(psu, hsT2[t][:, dc, :], buf[:, dc, :],
                              start=(dc == 0), stop=False)
                PE.matmul(psu, onesr[:], brow_qg[:, 256 * ob:256 * (ob + 1)],
                          start=False, stop=True)
            PE.bump(s_pe, 1)
            if ob % 2 == 1:
                for t in range(2):
                    seg = ob // 2
                    VE.wait(s_pe, s_pe.n)
                    if seg < 4:
                        VE.tensor_copy(out=qst[t][:, 512 * seg:512 * (seg + 1)],
                                       in_=PS[2 + t][:])
                    else:
                        VE.tensor_copy(
                            out=gatest[t][:, 512 * (seg - 4):512 * (seg - 3)],
                            in_=PS[2 + t][:])
                    VE.bump(s_v, 1)
                PE.wait(s_v, s_v.n)

        for t in range(2):
            T = 2 * G + t
            AC.wait(s_v, s_v.n)
            for seg in range(4):
                AC.activation(out=gatest[t][:, 512 * seg:512 * (seg + 1)],
                              in_=gatest[t][:, 512 * seg:512 * (seg + 1)],
                              func=AF.Sigmoid)
            AC.bump(s_a, 1)

            for half in range(2):
                qv = qst[t][:].rearrange("p (h d) -> p h d", h=H)[
                    :, 8 * half:8 * half + 8, :]
                VE.tensor_relu(xr2[:, :, 0:128], qv)
                VE.tensor_scalar(out=xr2[:, :, 128:256], in0=qv, scalar1=-1.0,
                                 scalar2=0.0, op0=ALU.mult, op1=ALU.max)
                VE.drain()
                for j in (1, 2, 3):
                    o = 256 * (j - 1)
                    VE.tensor_tensor(out=dpq32[:, :, o + j:o + 256],
                                     in0=xr2[:, :, j:256],
                                     in1=xr2[:, :, 0:256 - j], op=ALU.mult)
                    VE.tensor_tensor(out=dpq32[:, :, o:o + j],
                                     in0=xr2[:, :, 0:j],
                                     in1=xr2[:, :, 256 - j:256], op=ALU.mult)
                VE.drain()
                VE.tensor_copy(out=dpqb[:], in_=dpq32[:]
                               .rearrange("p a b -> p (a b)"))
                VE.bump(s_v, 1)
                AC.wait(s_v, s_v.n)
                for hh in range(8):
                    AC.activation(out=SCR[:, 0:FD], in_=dpq32[:, hh, :],
                                  func=AF.Square,
                                  accum_out=n2q[:, 8 * half + hh:
                                                8 * half + hh + 1])
                AC.bump(s_a, 1)
                PE.wait(s_v, s_v.n)
                for rnd in range(6):
                    for j in range(8):
                        c = rnd * 8 + j
                        PE.transpose(PSb[6][:, 128 * j:128 * (j + 1)],
                                     dpqb[:, 128 * c:128 * (c + 1)], ident[:])
                    PE.bump(s_pe, 1)
                    AC.wait(s_pe, s_pe.n)
                    AC.copy(out=dpqT[:, 8 * rnd:8 * rnd + 8, :]
                            .rearrange("p a b -> p (a b)"),
                            in_=PSb[6][:, 0:1024])
                    AC.bump(s_a, 1)
                    PE.wait(s_a, s_a.n)
                VE.wait(s_a, s_a.n)
                emit_rsqrt(VE, rsiq[:, 8 * half:8 * half + 8],
                           n2q[:, 8 * half:8 * half + 8],
                           scaq[:, 0:8], scbq[:, 0:8], floor=1e-24)
                VE.tensor_tensor(out=s1e8q[:, 8 * half:8 * half + 8],
                                 in0=n2q[:, 8 * half:8 * half + 8],
                                 in1=rsiq[:, 8 * half:8 * half + 8],
                                 op=ALU.mult)
                VE.drain()
                VE.tensor_scalar(out=s1e8q[:, 8 * half:8 * half + 8],
                                 in0=s1e8q[:, 8 * half:8 * half + 8],
                                 scalar1=1e-8, scalar2=None, op0=ALU.mult)
                VE.drain()
                VE.bump(s_v, 1)

                if half > 0 or t > 0 or G > 0:
                    PE.wait(s_v, s_v.n)
                for hh in range(8):
                    h = 8 * half + hh
                    pnum = PS[hh // 4][:, 128 * (hh % 4):128 * (hh % 4 + 1)]
                    for fc in range(FC):
                        lt = dpqT[:, hh * FC + fc, :]
                        PE.matmul(pnum, lt,
                                  mem_aug[:, h * FC + fc, :],
                                  start=(fc == 0), stop=(fc == FC - 1))
                        PE.matmul(PS[7][:, h:h + 1], lt,
                                  nrmcol[:, fc, h:h + 1],
                                  start=(fc == 0), stop=(fc == FC - 1))
                PE.bump(s_pe, 1)
                VE.wait(s_pe, s_pe.n)
                hsl = slice(8 * half, 8 * half + 8)
                VE.tensor_tensor(out=dDq[:, hsl], in0=PS[7][:, hsl],
                                 in1=s1e8q[:, hsl], op=ALU.add)
                VE.drain()
                VE.reciprocal(recDq[:, hsl], dDq[:, hsl])
                VE.drain()
                for hh in range(8):
                    h = 8 * half + hh
                    pnum = PS[hh // 4][:, 128 * (hh % 4):128 * (hh % 4 + 1)]
                    VE.tensor_scalar(out=mh[:, 128 * h:128 * (h + 1)],
                                     in0=pnum,
                                     scalar1=recDq[:, h:h + 1],
                                     scalar2=None, op0=ALU.mult)
                VE.drain()
                VE.bump(s_v, 1)

            VE.wait(s_a, s_a.n)
            VE.tensor_tensor(out=mh[:], in0=mh[:], in1=gatest[t][:],
                             op=ALU.mult)
            VE.drain()
            VE.tensor_scalar(out=SCR[:], in0=xt2b[T % 2][:],
                             scalar1=ostat_m[:, T:T + 1],
                             scalar2=ostat_r[:, T:T + 1],
                             op0=ALU.subtract, op1=ALU.mult)
            VE.drain()
            VE.tensor_tensor(out=mh[:], in0=mh[:], in1=SCR[:], op=ALU.add)
            VE.drain()
            VE.tensor_tensor(out=mh[:], in0=mh[:], in1=xt2b[T % 2][:],
                             op=ALU.add)
            VE.drain()
            VE.reduce_sum(stats16[:, T, 0:1], mh[:], axis=AX.X)
            VE.drain()
            VE.tensor_scalar(out=stats16[:, T, 0:1], in0=stats16[:, T, 0:1],
                             scalar1=1.0 / D, scalar2=None, op0=ALU.mult)
            VE.drain()
            VE.tensor_scalar(out=SCR[:], in0=mh[:],
                             scalar1=stats16[:, T, 0:1],
                             scalar2=None, op0=ALU.subtract)
            VE.drain()
            VE.tensor_tensor(out=SCR[:], in0=SCR[:], in1=SCR[:], op=ALU.mult)
            VE.drain()
            VE.reduce_sum(stats16[:, T, 1:2], SCR[:], axis=AX.X)
            VE.bump(s_v, 1)
            SP.wait(s_v, s_v.n)
            SP.dma_start(out=xb_t[T], in_=mh[:]).then_inc(d_st, 16)
            if "x1" in taps:
                SP.dma_start(out=taps["x1"].rearrange("(n p) d -> n p d",
                                                      p=128)[T],
                             in_=mh[:]).then_inc(d_out, 16)
            VE.wait(d_st, d_st.n)

    VE.tensor_scalar(out=stats16[:, :, 1], in0=stats16[:, :, 1],
                     scalar1=1.0 / D, scalar2=1e-5,
                     op0=ALU.mult, op1=ALU.add)
    VE.drain()
    emit_rsqrt(VE, stats16[:, :, 1], stats16[:, :, 1], oscr1[:], oscr2[:])
    VE.bump(s_v, 1)
    ckpt("p2")
    ar.release()
    ar.release()

    def emit_ffn(tag, dest_t, do_stats):
        ar.mark()
        hsT3 = [ar.alloc(f"{tag}hsT{i}", [128, 16, 128], BF16)
                for i in range(4)]
        hsb3 = ar.alloc(f"{tag}hsb", [128, D], BF16)
        wguf = [ar.alloc(f"{tag}wgu{i}", [128, 16, 512], BF16)
                for i in range(2)]
        gtmp = [ar.alloc(f"{tag}gt{i}", [128, 1024], F32) for i in range(4)]
        hbuf = [ar.alloc(f"{tag}h{i}", [128, 1024], F32) for i in range(4)]
        habs = ar.alloc(f"{tag}habs", [128, 1024], F32)
        hq = [ar.alloc(f"{tag}hq{i}", [128, 256], F32) for i in range(5)]
        hbf1 = ar.alloc(f"{tag}hbf", [128, 1024], BF16)
        hbf = [hbf1] * 4
        hT = [ar.alloc(f"{tag}hT{i}", [128, 8, 128], BF16) for i in range(4)]
        wdbuf = [ar.alloc(f"{tag}wd{i}", [128, 8, 512], BF16)
                 for i in range(2)]
        dacc = [ar.alloc(f"{tag}da{i}", [128, D], F32) for i in range(4)]
        fsc1 = ar.alloc(f"{tag}fsc1", [128, NT_OWN], F32)
        fsc2 = ar.alloc(f"{tag}fsc2", [128, NT_OWN], F32)

        phase_barrier()
        for _sm, _vv in CACHE_T["wdf_c"]:
            PE.wait(_sm, _vv)
        for G in range(4):
            VE.wait(s_a, s_a.n)
            for t in range(4):
                T = 4 * G + t
                SP.wait(s_v, s_v.n)
                SP.dma_start(out=dacc[t][:], in_=xb_t[T]
                             ).then_inc(d_xp[t % 2], 16)
                VE.wait(d_xp[t % 2], d_xp[t % 2].n)
                VE.tensor_scalar(out=SCR[:], in0=dacc[t][:],
                                 scalar1=stats16[:, T, 0:1],
                                 scalar2=stats16[:, T, 1:2],
                                 op0=ALU.subtract, op1=ALU.mult)
                VE.tensor_copy(out=hsb3[:], in_=SCR[:])
                VE.bump(s_v, 1)
                PE.wait(s_v, s_v.n)
                for half in range(2):
                    for j in range(8):
                        c = half * 8 + j
                        PE.transpose(PSb[6][:, 128 * j:128 * (j + 1)],
                                     hsb3[:, 128 * c:128 * (c + 1)],
                                     ident[:])
                    PE.bump(s_pe, 1)
                    AC.wait(s_pe, s_pe.n)
                    AC.copy(out=hsT3[t][:, 8 * half:8 * half + 8, :]
                            .rearrange("p a b -> p (a b)"),
                            in_=PSb[6][:, 0:1024])
                    AC.bump(s_a, 1)
                    PE.wait(s_a, s_a.n)
                VE.wait(s_a, s_a.n)

            def w_issue(ps):
                if ps is None:
                    return None
                if ps[0] == "gu":
                    _, IG_, gu_, p2_ = ps
                    slot = (IG_ * 4 + gu_ * 2 + p2_) % 2
                    wsrc = wgf_c if gu_ == 0 else wuf_c
                    cols = slice(IG_ * 1024 + 512 * p2_,
                                 IG_ * 1024 + 512 * (p2_ + 1))
                    SP.wait(s_pe, s_pe.n)
                    SP.dma_start(
                        out=wguf[slot][:],
                        in_=wsrc.ap()[:, cols]
                        .rearrange("(dc p) o -> p dc o", p=128)
                        ).then_inc(d_wp[slot], 16)
                    return (slot, d_wp[slot].n)
                _, IG_, dq_ = ps
                slot = (IG_ * 4 + dq_) % 2
                i0_ = IG_ * 1024
                SP.wait(s_pe, s_pe.n)
                SP.dma_start(
                    out=wdbuf[slot][:],
                    in_=wdf_c.ap()[i0_:i0_ + 1024,
                                   512 * dq_:512 * (dq_ + 1)]
                    .rearrange("(a p) b -> p a b", p=128)
                    ).then_inc(d_wp[slot], 16)
                return (slot, d_wp[slot].n)

            sched = []
            for IG in range(8):
                for gu in range(2):
                    for p2 in range(2):
                        sched.append(("gu", IG, gu, p2))
                for dq in range(4):
                    sched.append(("dn", IG, dq))
            pend = {0: w_issue(sched[0])}
            si = 0
            for IG in range(8):
                i0 = IG * 1024
                for gu in range(2):
                    for p2 in range(2):
                        buf = wguf[pend[si][0]]
                        if si + 1 < len(sched):
                            pend[si + 1] = w_issue(sched[si + 1])
                        PE.wait(d_wp[pend[si][0]], pend[si][1])
                        si += 1
                        PE.wait(s_v, s_v.n)
                        PE.wait(s_a, s_a.n)
                        for t in range(4):
                            psu = PS[2 + t][:]
                            for dc in range(16):
                                PE.matmul(psu, hsT3[t][:, dc, :],
                                          buf[:, dc, :],
                                          start=(dc == 0), stop=(dc == 15))
                        PE.bump(s_pe, 1)
                        for t in range(4):
                            if gu == 0:
                                AC.wait(s_pe, s_pe.n)
                                AC.activation(
                                    out=gtmp[t][:, 512 * p2:512 * (p2 + 1)],
                                    in_=PS[2 + t][:], func=AF.Silu)
                                AC.bump(s_a, 1)
                            else:
                                VE.wait(s_pe, s_pe.n)
                                VE.wait(s_a, s_a.n)
                                VE.tensor_tensor(
                                    out=hbuf[t][:, 512 * p2:512 * (p2 + 1)],
                                    in0=PS[2 + t][:],
                                    in1=gtmp[t][:, 512 * p2:512 * (p2 + 1)],
                                    op=ALU.mult)
                                VE.bump(s_v, 1)

                for t in range(4):
                    h = hbuf[t][:]
                    hg = h.rearrange("p (g four) -> p g four", four=4)
                    VE.tensor_scalar(out=habs[:], in0=h, scalar1=-1.0,
                                     scalar2=None, op0=ALU.mult)
                    VE.drain()
                    VE.tensor_tensor(out=habs[:], in0=habs[:], in1=h,
                                     op=ALU.max)
                    VE.drain()
                    ag = habs[:].rearrange("p (g four) -> p g four", four=4)
                    VE.tensor_tensor(out=hq[0][:], in0=ag[:, :, 0],
                                     in1=ag[:, :, 1], op=ALU.max)
                    VE.tensor_tensor(out=hq[1][:], in0=ag[:, :, 0],
                                     in1=ag[:, :, 1], op=ALU.min)
                    VE.tensor_tensor(out=hq[2][:], in0=ag[:, :, 2],
                                     in1=ag[:, :, 3], op=ALU.max)
                    VE.tensor_tensor(out=hq[3][:], in0=ag[:, :, 2],
                                     in1=ag[:, :, 3], op=ALU.min)
                    VE.drain()
                    VE.tensor_tensor(out=hq[0][:], in0=hq[0][:], in1=hq[2][:],
                                     op=ALU.min)
                    VE.tensor_tensor(out=hq[1][:], in0=hq[1][:], in1=hq[3][:],
                                     op=ALU.max)
                    VE.drain()
                    VE.tensor_tensor(out=hq[4][:], in0=hq[0][:], in1=hq[1][:],
                                     op=ALU.max)
                    VE.drain()
                    hbg = hbf[t][:].rearrange("p (g four) -> p g four",
                                              four=4)
                    for q in range(4):
                        VE.tensor_tensor(out=hq[q][:], in0=ag[:, :, q],
                                         in1=hq[4][:], op=ALU.is_ge)
                    VE.drain()
                    VE.wait(s_a, s_a.n)
                    for q in range(4):
                        VE.tensor_tensor(out=hbg[:, :, q], in0=hg[:, :, q],
                                         in1=hq[q][:], op=ALU.mult)
                    VE.bump(s_v, 1)
                    PE.wait(s_v, s_v.n)
                    for j in range(8):
                        PE.transpose(PSb[6][:, 128 * j:128 * (j + 1)],
                                     hbf[t][:, 128 * j:128 * (j + 1)],
                                     ident[:])
                    PE.bump(s_pe, 1)
                    AC.wait(s_pe, s_pe.n)
                    AC.copy(out=hT[t][:].rearrange("p a b -> p (a b)"),
                            in_=PSb[6][:, 0:1024])
                    AC.bump(s_a, 1)
                    PE.wait(s_a, s_a.n)

                for dq in range(4):
                    buf = wdbuf[pend[si][0]]
                    if si + 1 < len(sched):
                        pend[si + 1] = w_issue(sched[si + 1])
                    PE.wait(d_wp[pend[si][0]], pend[si][1])
                    si += 1
                    if IG > 0 or dq > 0:
                        PE.wait(s_v, s_v.n)
                    for t in range(4):
                        for ic in range(8):
                            PE.matmul(PS[t][:], hT[t][:, ic, :],
                                      buf[:, ic, :],
                                      start=(ic == 0), stop=(ic == 7))
                    PE.bump(s_pe, 1)
                    VE.wait(s_pe, s_pe.n)
                    for t in range(4):
                        VE.tensor_tensor(
                            out=dacc[t][:, 512 * dq:512 * (dq + 1)],
                            in0=PS[t][:],
                            in1=dacc[t][:, 512 * dq:512 * (dq + 1)],
                            op=ALU.add)
                    VE.bump(s_v, 1)

            for t in range(4):
                T = 4 * G + t
                VE.reduce_sum(stats16[:, T, 0:1], dacc[t][:], axis=AX.X)
                VE.drain()
                VE.tensor_scalar(out=stats16[:, T, 0:1],
                                 in0=stats16[:, T, 0:1],
                                 scalar1=1.0 / D, scalar2=None, op0=ALU.mult)
                VE.drain()
                VE.tensor_scalar(out=SCR[:], in0=dacc[t][:],
                                 scalar1=stats16[:, T, 0:1],
                                 scalar2=None, op0=ALU.subtract)
                VE.drain()
                VE.tensor_tensor(out=SCR[:], in0=SCR[:], in1=SCR[:],
                                 op=ALU.mult)
                VE.drain()
                VE.reduce_sum(stats16[:, T, 1:2], SCR[:], axis=AX.X)
                VE.bump(s_v, 1)
                SP.wait(s_v, s_v.n)
                SP.dma_start(out=dest_t[T], in_=dacc[t][:]).then_inc(d_st, 16)
            SP.wait(d_st, d_st.n)
            VE.wait(d_st, d_st.n)
        if do_stats:
            VE.tensor_scalar(out=stats16[:, :, 1],
                             in0=stats16[:, :, 1],
                             scalar1=1.0 / D, scalar2=1e-5,
                             op0=ALU.mult, op1=ALU.add)
            VE.drain()
            emit_rsqrt(VE, stats16[:, :, 1], stats16[:, :, 1],
                       fsc1[:], fsc2[:])
            VE.bump(s_v, 1)
        ar.release()

    emit_ffn("f1", xb_t, True)
    ckpt("p3")

    ar.mark()
    cosq_sb = ar.alloc("cosq_sb", [128, 16, 32], F32)
    sinq_sb = ar.alloc("sinq_sb", [128, 16, 32], F32)
    cosk_sb = ar.alloc("cosk_sb", [128, 16, 32], F32)
    sink_sb = ar.alloc("sink_sb", [128, 16, 32], F32)
    sublnw_sb = ar.alloc("sublnw_sb", [128, 128], F32)
    lam_sb = ar.alloc("lam_sb", [128, 256], F32)
    neglam = ar.alloc("neglam", [128, 1], F32)
    lsc1 = ar.alloc("lsc1", [128, 1], F32)
    lsc2 = ar.alloc("lsc2", [128, 1], F32)

    phase_barrier()
    for _sm, _vv in CACHE_T["wo_c"]:
        PE.wait(_sm, _vv)

    SP.dma_start(out=cosq_sb[:], in_=cosq.rearrange("(T p) i -> p T i", p=128)
                 ).then_inc(d_x, 16)
    SP.dma_start(out=sinq_sb[:], in_=sinq.rearrange("(T p) i -> p T i", p=128)
                 ).then_inc(d_x, 16)
    SP.dma_start(out=cosk_sb[:], in_=cosk.rearrange("(T p) i -> p T i", p=128)
                 ).then_inc(d_x, 16)
    SP.dma_start(out=sink_sb[:], in_=sink.rearrange("(T p) i -> p T i", p=128)
                 ).then_inc(d_x, 16)
    SP.dma_start(out=sublnw_sb[:], in_=sublnw_r).then_inc(d_x, 16)
    SP.dma_start(out=lam_sb[:], in_=lam_in).then_inc(d_x, 16)
    PH4_CONST = d_x.n

    VE.wait(d_x, PH4_CONST)
    VE.tensor_tensor(out=lam_sb[:, 0:64], in0=lam_sb[:, 0:64],
                     in1=lam_sb[:, 128:192], op=ALU.mult)
    VE.drain()
    VE.reduce_sum(lsc1[:], lam_sb[:, 0:64], axis=AX.X)
    VE.drain()
    VE.tensor_tensor(out=lam_sb[:, 64:128], in0=lam_sb[:, 64:128],
                     in1=lam_sb[:, 192:256], op=ALU.mult)
    VE.drain()
    VE.reduce_sum(lsc2[:], lam_sb[:, 64:128], axis=AX.X)
    VE.bump(s_v, 1)
    AC.wait(s_v, s_v.n)
    AC.activation(out=lsc1[:], in_=lsc1[:], func=AF.Exp)
    AC.activation(out=lsc2[:], in_=lsc2[:], func=AF.Exp)
    AC.bump(s_a, 1)
    VE.wait(s_a, s_a.n)
    VE.tensor_tensor(out=neglam[:], in0=lsc2[:], in1=lsc1[:], op=ALU.subtract)
    VE.drain()
    VE.tensor_scalar(out=neglam[:], in0=neglam[:], scalar1=-LAMBDA_INIT,
                     scalar2=None, op0=ALU.add)
    VE.bump(s_v, 1)

    ar.mark()
    xt4 = [ar.alloc(f"a4x{i}", [128, D], F32) for i in range(2)]
    hsb4 = ar.alloc("a4hsb", [128, D], BF16)
    hsT4 = [ar.alloc(f"a4hsT{i}", [128, 16, 128], BF16) for i in range(2)]
    wbuf4 = [ar.alloc(f"a4w{i}", [128, 16, 512], BF16) for i in range(2)]
    qkv32 = [ar.alloc(f"a4qkv{i}", [128, 2, D], F32) for i in range(2)]
    vbf4 = [ar.alloc(f"a4vbf{i}", [128, D], BF16) for i in range(2)]
    rp = [ar.alloc(f"a4rp{i}", [128, 1024], F32) for i in range(4)]
    qkbf = ar.alloc("a4qkbf", [128, D], BF16)
    tstg = [ar.alloc(f"a4st{i}", [128, 8, 128], BF16) for i in range(2)]
    d_sh = [mksem("d_sh0"), mksem("d_sh1")]

    def bcast32(tab, T):
        return tab[:, T:T + 1, :].to_broadcast((128, 32, 32))

    for G in range(8):
        VE.wait(s_a, s_a.n)
        for t in range(2):
            T = 2 * G + t
            SP.wait(s_v, s_v.n)
            SP.dma_start(out=xt4[t][:], in_=xb_t[T]).then_inc(d_xp[t], 16)
            VE.wait(d_xp[t], d_xp[t].n)
            VE.tensor_scalar(out=SCR[:], in0=xt4[t][:],
                             scalar1=stats16[:, T, 0:1],
                             scalar2=stats16[:, T, 1:2],
                             op0=ALU.subtract, op1=ALU.mult)
            VE.tensor_copy(out=hsb4[:], in_=SCR[:])
            VE.bump(s_v, 1)
            PE.wait(s_v, s_v.n)
            for half in range(2):
                for j in range(8):
                    c = half * 8 + j
                    PE.transpose(PSb[6][:, 128 * j:128 * (j + 1)],
                                 hsb4[:, 128 * c:128 * (c + 1)], ident[:])
                PE.bump(s_pe, 1)
                AC.wait(s_pe, s_pe.n)
                AC.copy(out=hsT4[t][:, 8 * half:8 * half + 8, :]
                        .rearrange("p a b -> p (a b)"), in_=PSb[6][:, 0:1024])
                AC.bump(s_a, 1)
                PE.wait(s_a, s_a.n)
            VE.wait(s_a, s_a.n)

        def qkv_issue(ob_):
            SP.wait(s_pe, s_pe.n)
            SP.dma_start(
                out=wbuf4[ob_ % 2][:],
                in_=wqkvb_c.ap()[:, 512 * ob_:512 * (ob_ + 1)]
                .rearrange("(dc p) o -> p dc o", p=128)
                ).then_inc(d_wp[ob_ % 2], 16)
            return d_wp[ob_ % 2].n
        qkv_pend = {0: qkv_issue(0)}
        for ob in range(12):
            buf = wbuf4[ob % 2]
            if ob + 1 < 12:
                qkv_pend[ob + 1] = qkv_issue(ob + 1)
            PE.wait(d_wp[ob % 2], qkv_pend[ob])
            PE.wait(s_v, s_v.n)
            for t in range(2):
                psu = PS[2 + t][:]
                for dc in range(16):
                    PE.matmul(psu, hsT4[t][:, dc, :], buf[:, dc, :],
                              start=(dc == 0), stop=(dc == 15))
            PE.bump(s_pe, 1)
            for t in range(2):
                VE.wait(s_pe, s_pe.n)
                part, off = ob // 4, 512 * (ob % 4)
                if part < 2:
                    VE.tensor_copy(out=qkv32[t][:, part, off:off + 512],
                                   in_=PS[2 + t][:])
                else:
                    VE.tensor_copy(out=vbf4[t][:, off:off + 512],
                                   in_=PS[2 + t][:])
                VE.bump(s_v, 1)
            PE.wait(s_v, s_v.n)

        for t in range(2):
            T = 2 * G + t
            for part, (ctab, stab) in ((0, (cosq_sb, sinq_sb)),
                                       (1, (cosk_sb, sink_sb))):
                pv = qkv32[t][:, part, :].rearrange(
                    "p (h i two) -> p h i two", i=32, two=2)
                ev, od = pv[:, :, :, 0], pv[:, :, :, 1]
                cB, sB = bcast32(ctab, T), bcast32(stab, T)
                r0 = rp[0][:].rearrange("p (h i) -> p h i", i=32)
                r1 = rp[1][:].rearrange("p (h i) -> p h i", i=32)
                r2 = rp[2][:].rearrange("p (h i) -> p h i", i=32)
                r3 = rp[3][:].rearrange("p (h i) -> p h i", i=32)
                VE.tensor_tensor(out=r0, in0=ev, in1=cB, op=ALU.mult)
                VE.tensor_tensor(out=r1, in0=od, in1=sB, op=ALU.mult)
                VE.tensor_tensor(out=r2, in0=ev, in1=sB, op=ALU.mult)
                VE.tensor_tensor(out=r3, in0=od, in1=cB, op=ALU.mult)
                VE.drain()
                obv = qkbf[:].rearrange("p (h i two) -> p h i two",
                                        i=32, two=2)
                VE.tensor_tensor(out=obv[:, :, :, 0], in0=r0, in1=r1,
                                 op=ALU.subtract)
                VE.tensor_tensor(out=obv[:, :, :, 1], in0=r2, in1=r3,
                                 op=ALU.add)
                VE.bump(s_v, 1)
                dst_d = qT_d if part == 0 else kT_d
                sh_plane = 0 if part == 0 else 1
                PE.wait(s_v, s_v.n)
                for half in range(2):
                    stg4 = tstg[half]
                    SP.wait(d_sh[half], d_sh[half].n)
                    for j in range(8):
                        c = half * 8 + j
                        PE.transpose(PSb[6][:, 128 * j:128 * (j + 1)],
                                     qkbf[:, 128 * c:128 * (c + 1)],
                                     ident[:])
                    PE.bump(s_pe, 1)
                    AC.wait(s_pe, s_pe.n)
                    AC.wait(d_sh[half], d_sh[half].n)
                    AC.copy(out=stg4[:].rearrange("p a b -> p (a b)"),
                            in_=PSb[6][:, 0:1024])
                    AC.bump(s_a, 1)
                    PE.wait(s_a, s_a.n)
                    SP.wait(s_a, s_a.n)
                    if half == 0:
                        SP.dma_start(
                            out=dst_d.ap()
                            .rearrange("(c p) t -> p c t", p=128)
                            [:, :, 128 * T:128 * (T + 1)],
                            in_=stg4[:]).then_inc(d_sh[half], 16)
                    else:
                        def mk_sh(stg_=stg4, pl_=sh_plane, T_=T, hf_=half):
                            def go(g_):
                                rpar = PLregs["rpar"]
                                for par in range(2):
                                    ctx = (g_.If_eq(rpar, 0) if par == 0
                                           else g_.Else())
                                    with ctx:
                                        g_.dma_start(
                                            out=sh_qkv[par, pl_]
                                            .rearrange("(c p) t -> p c t",
                                                       p=128)
                                            [:, :, 128 * T_:128 * (T_ + 1)],
                                            in_=stg_[:]
                                            ).then_inc(d_sh[hf_].h, 16)
                            return go
                        PL.wait(s_a, s_a.n)
                        PL.raw(mk_sh())
                        d_sh[half].n += 16
            SP.wait(d_sh[0], d_sh[0].n)
            SP.dma_start(out=v_d.ap()[128 * T:128 * (T + 1), :],
                         in_=vbf4[t][:, 0:1024]).then_inc(d_sh[0], 16)

            def mk_shv(T_=T):
                def go(g_):
                    rpar = PLregs["rpar"]
                    for par in range(2):
                        ctx = (g_.If_eq(rpar, 0) if par == 0 else g_.Else())
                        with ctx:
                            g_.dma_start(
                                out=sh_qkv[par, 2].rearrange("a b -> (a b)")
                                [1024 * 128 * T_:1024 * 128 * (T_ + 1)]
                                .rearrange("(p f) -> p f", p=128),
                                in_=vbf4[T_ % 2][:, 1024:2048]
                                ).then_inc(d_sh[1].h, 16)
                return go
            PL.wait(d_sh[1], d_sh[1].n)
            PL.raw(mk_shv())
            d_sh[1].n += 16
            VE.wait(d_sh[0], d_sh[0].n)
            VE.wait(d_sh[1], d_sh[1].n)
    SP.wait(d_sh[0], d_sh[0].n)
    PL.wait(d_sh[1], d_sh[1].n)
    ar.release()
    ckpt("p4a")
    pair_barrier()
    phase_barrier()

    ar.mark()
    kTs = ar.alloc("kTs", [128, 8, SEQ], BF16)
    vas = ar.alloc("vas", [128, 32, 1032], BF16)
    qTt = [ar.alloc(f"qTt{i}", [128, 8, 512], BF16) for i in range(2)]
    eTs = ar.alloc("eTs", [128, 32, 512], BF16)
    pstg = ar.alloc("pstg", [128, 512], BF16)
    d_m = [mksem("d_m0"), mksem("d_m1")]
    d_at = mksem("d_at")
    d_ao = mksem("d_ao")
    lbuf = ar.alloc("lbuf", [128, 8], F32)
    rlbuf = ar.alloc("rlbuf", [128, 8], F32)
    rn1 = ar.alloc("rn1", [128, 4], F32)
    ssb = ar.alloc("ssb", [128, 4], F32)
    at32 = [ar.alloc(f"at32_{i}", [128, 128], F32) for i in range(4)]
    abf = ar.alloc("abf", [128, 128], BF16)
    ar.mark()
    mcw = ar.alloc("mcw", [128, 16, 4, 8], F32)
    mcw2 = ar.alloc("mcw2", [128, 16, 4, 4], F32)
    mcw3 = ar.alloc("mcw3", [128, 16, 4, 2], F32)
    mcw4 = ar.alloc("mcw4", [128, 64], F32)
    mneg = ar.alloc("mneg", [128, 64], BF16)
    mstg = ar.alloc("mstg", [64, 128], F32)
    d_q = [mksem("d_q0"), mksem("d_q1")]
    d_qp = [mksem("d_qp0"), mksem("d_qp1")]
    d_kv = mksem("d_kv")
    d_kvp = mksem("d_kvp")

    SP.dma_start(out=kTs[:, :, 0:ROWS],
                 in_=kT_d.ap().rearrange("(c p) t -> p c t", p=128)
                 ).then_inc(d_kv, 16)

    def mk_peer_loads():
        def go(g_):
            rpar = PLregs["rpar"]
            for par in range(2):
                src_par = 1 - par
                ctx = (g_.If_eq(rpar, 0) if par == 0 else g_.Else())
                with ctx:
                    g_.dma_start(
                        out=kTs[:, :, ROWS:SEQ],
                        in_=sh_qkv[src_par, 1]
                        .rearrange("(c p) t -> p c t", p=128)
                        ).then_inc(d_kvp.h, 16)
                    vsrc = (sh_qkv[src_par, 2].rearrange("a b -> (a b)")
                            .rearrange("(t f) -> t f", f=1024))
                    for tc in range(16):
                        g_.dma_start(
                            out=vas[:, 16 + tc, :]
                            .rearrange("p (v d) -> p v d", d=129)
                            [:, :, 0:128],
                            in_=vsrc[128 * tc:128 * (tc + 1), :]
                            .rearrange("p (v d) -> p v d", d=128)
                            ).then_inc(d_kvp.h, 16)
        return go

    VE.memset(vas[:], 1.0)
    VE.drain()
    VE.bump(s_v, 1)
    SP.wait(s_v, s_v.n)
    for tc in range(16):
        SP.dma_start(
            out=vas[:, tc, :].rearrange("p (v d) -> p v d", d=129)
            [:, :, 0:128],
            in_=v_d.ap()[128 * tc:128 * (tc + 1), :]
            .rearrange("p (v d) -> p v d", d=128)).then_inc(d_kv, 16)
    PL.wait(s_v, s_v.n)
    PL.raw(mk_peer_loads())
    d_kvp.n += 16 * 17
    KV_READY = (d_kv.n, d_kvp.n)


    m_blk = 0
    m_bank_free = {b: None for b in range(6)}
    for Qb in range(8):
        qb_buf = qTt[Qb % 2]
        if Qb < 4:
            SP.wait(d_q[Qb % 2], d_q[Qb % 2].n)
            SP.wait(d_qp[Qb % 2], d_qp[Qb % 2].n)
            SP.wait(s_pe, s_pe.n)
            SP.dma_start(
                out=qb_buf[:],
                in_=qT_d.ap().rearrange("(c p) t -> p c t", p=128)
                [:, :, 512 * Qb:512 * (Qb + 1)]
                ).then_inc(d_q[Qb % 2], 16)
        else:
            def mk_qload_m(dst=qb_buf, Qb_=Qb, par_i=(Qb % 2)):
                def go(g_):
                    rpar = PLregs["rpar"]
                    for par in range(2):
                        src_par = 1 - par
                        ctx = (g_.If_eq(rpar, 0) if par == 0 else g_.Else())
                        with ctx:
                            g_.dma_start(
                                out=dst[:],
                                in_=sh_qkv[src_par, 0]
                                .rearrange("(c p) t -> p c t", p=128)
                                [:, :, 512 * (Qb_ - 4):512 * (Qb_ - 3)]
                                ).then_inc(d_qp[par_i].h, 16)
                return go
            PL.wait(d_qp[Qb % 2], d_qp[Qb % 2].n)
            PL.wait(s_pe, s_pe.n)
            PL.raw(mk_qload_m())
            d_qp[Qb % 2].n += 16
        if Qb < 4:
            PE.wait(d_q[Qb % 2], d_q[Qb % 2].n)
        else:
            PE.wait(d_qp[Qb % 2], d_qp[Qb % 2].n)
        if Qb == 0:
            PE.wait(d_kv, KV_READY[0])
            PE.wait(d_kvp, KV_READY[1])
        for hh in range(16):
            chunk, base = hh // 2, 64 * (hh % 2)
            for j in range(4):
                lhsT = qb_buf[base:base + 64, chunk, 128 * j:128 * (j + 1)]
                for sb2 in range(8):
                    mb_ = m_blk % 6
                    if m_bank_free[mb_] is not None:
                        PE.wait(s_v, m_bank_free[mb_])
                    PE.matmul(PS[mb_][:], lhsT,
                              kTs[base:base + 64, chunk,
                                  512 * sb2:512 * (sb2 + 1)],
                              start=True, stop=True)
                    PE.bump(s_pe, 1)
                    VE.wait(s_pe, s_pe.n)
                    VE.reduce_max(mcw[:, hh, j, sb2:sb2 + 1], PS[mb_][:],
                                  axis=AX.X)
                    VE.bump(s_v, 1)
                    m_bank_free[mb_] = s_v.n
                    m_blk += 1
        VE.tensor_tensor(out=mcw2[:], in0=mcw[:, :, :, 0:4],
                         in1=mcw[:, :, :, 4:8], op=ALU.max)
        VE.tensor_tensor(out=mcw3[:], in0=mcw2[:, :, :, 0:2],
                         in1=mcw2[:, :, :, 2:4], op=ALU.max)
        VE.tensor_tensor(out=mcw4[:].rearrange("p (a b c) -> p a b c",
                                               a=16, b=4),
                         in0=mcw3[:, :, :, 0:1],
                         in1=mcw3[:, :, :, 1:2], op=ALU.max)
        VE.tensor_scalar(out=mneg[:], in0=mcw4[:],
                         scalar1=-1.0, scalar2=None, op0=ALU.mult)
        VE.bump(s_v, 1)
        PE.wait(s_v, s_v.n)
        PE.wait(s_a, s_a.n)
        PE.transpose(PSb[6][0:64, 0:128], mneg[:], ident[:])
        PE.bump(s_pe, 1)
        AC.wait(s_pe, s_pe.n)
        AC.wait(d_m[0], d_m[0].n)
        AC.copy(out=mstg[:], in_=PSb[6][0:64, 0:128])
        AC.bump(s_a, 1)
        PE.wait(s_a, s_a.n)
        SP.wait(s_a, s_a.n)
        for hh in range(16):
            SP.dma_start(out=md_d.ap()[hh, 512 * Qb:512 * (Qb + 1)]
                         .rearrange("(j t) -> j t", j=4),
                         in_=mstg[4 * hh:4 * hh + 4, :]).then_inc(d_m[0], 16)
    if os.environ.get("KTAP") == "md":
        SP.wait(d_m[0], d_m[0].n)
        PL.wait(d_m[0], d_m[0].n)
        PL.dma_start(out=y.rearrange("a b -> (a b)")[0:16 * SEQ]
                     .rearrange("(p w) -> p w", p=16),
                     in_=md_d.ap()).then_inc(d_pl, 16)
        PL.wait(d_pl, d_pl.n)
    ckpt("p4m")
    ar.release()
    nM = [ar.alloc(f"nM{i}", [128, 2, 512], F32) for i in range(2)]
    sscr = [PS[4], PS[5]]

    sv_bank = [None, None]
    exp_free = [None, None]
    for vh in range(8):
        c_hh = vh
        for Qb in range(8):
            qb_buf = qTt[Qb % 2]
            if Qb < 4:
                SP.wait(d_q[Qb % 2], d_q[Qb % 2].n)
                SP.wait(d_qp[Qb % 2], d_qp[Qb % 2].n)
                SP.wait(s_pe, s_pe.n)
                SP.dma_start(
                    out=qb_buf[:],
                    in_=qT_d.ap().rearrange("(c p) t -> p c t", p=128)
                    [:, :, 512 * Qb:512 * (Qb + 1)]
                    ).then_inc(d_q[Qb % 2], 16)
            else:
                def mk_qload(dst=qb_buf, Qb_=Qb, par_i=(Qb % 2)):
                    def go(g_):
                        rpar = PLregs["rpar"]
                        for par in range(2):
                            src_par = 1 - par
                            ctx = (g_.If_eq(rpar, 0) if par == 0
                                   else g_.Else())
                            with ctx:
                                g_.dma_start(
                                    out=dst[:],
                                    in_=sh_qkv[src_par, 0]
                                    .rearrange("(c p) t -> p c t", p=128)
                                    [:, :, 512 * (Qb_ - 4):512 * (Qb_ - 3)]
                                    ).then_inc(d_qp[par_i].h, 16)
                    return go
                PL.wait(d_qp[Qb % 2], d_qp[Qb % 2].n)
                PL.wait(s_pe, s_pe.n)
                PL.raw(mk_qload())
                d_qp[Qb % 2].n += 16
            if Qb < 4:
                PE.wait(d_q[Qb % 2], d_q[Qb % 2].n)
            else:
                PE.wait(d_qp[Qb % 2], d_qp[Qb % 2].n)
            if vh == 0 and Qb == 0:
                PE.wait(d_kv, KV_READY[0])
                PE.wait(d_kvp, KV_READY[1])
            PE.wait(s_v, s_v.n)

            SP.wait(d_m[Qb % 2], d_m[Qb % 2].n)
            SP.wait(s_v, s_v.n)
            for hh in range(2):
                SP.dma_start(out=nM[Qb % 2][:, hh, :],
                             in_=md_d.ap()[2 * vh + hh,
                                           512 * Qb:512 * (Qb + 1)][None, :]
                             .to_broadcast((128, 512))
                             ).then_inc(d_m[Qb % 2], 16)
            VE.wait(d_m[Qb % 2], d_m[Qb % 2].n)
            for hh in range(2):
                base = 64 * hh
                for sc in range(32):
                    sb_ = 6 + (sc % 2)
                    if sv_bank[sc % 2] is not None:
                        PE.wait(s_v, sv_bank[sc % 2])
                    PE.matmul(PS[sb_][:],
                              kTs[base:base + 64, c_hh,
                                  128 * sc:128 * (sc + 1)],
                              qb_buf[base:base + 64, c_hh, :],
                              start=True, stop=True)
                    PE.bump(s_pe, 1)
                    VE.wait(s_pe, s_pe.n)
                    if exp_free[sc % 2] is not None:
                        VE.wait(s_a, exp_free[sc % 2])
                    VE.tensor_tensor(out=sscr[sc % 2][:], in0=PS[sb_][:],
                                     in1=nM[Qb % 2][:, hh, :], op=ALU.add)
                    VE.bump(s_v, 1)
                    sv_bank[sc % 2] = s_v.n
                    AC.wait(s_v, s_v.n)
                    AC.activation(out=eTs[:, sc, :], in_=sscr[sc % 2][:],
                                  func=AF.Exp)
                    AC.bump(s_a, 1)
                    exp_free[sc % 2] = s_a.n
                EXP_DONE = s_a.n
                PE.wait(s_a, EXP_DONE)
                for sub in range(4):
                    idx = hh * 4 + sub
                    bank, off = idx // 3, 129 * (idx % 3)
                    for sc in range(32):
                        PE.matmul(PS[bank][:, off:off + 129],
                                  eTs[:, sc, 128 * sub:128 * (sub + 1)],
                                  vas[:, sc, 129 * vh:129 * (vh + 1)],
                                  start=(sc == 0), stop=(sc == 31))
                PE.bump(s_pe, 1)
                AC.wait(s_pe, s_pe.n)

            VE.wait(s_pe, s_pe.n)
            for idx in range(8):
                bank, off = idx // 3, 129 * (idx % 3)
                VE.tensor_copy(out=lbuf[:, idx:idx + 1],
                               in_=PS[bank][:, off + 128:off + 129])
            VE.drain()
            VE.reciprocal(rlbuf[:], lbuf[:])
            VE.drain()
            VE.wait(s_v, s_v.n)
            VE.tensor_scalar(out=rn1[:], in0=rlbuf[:, 4:8],
                             scalar1=neglam[:], scalar2=None, op0=ALU.mult)
            VE.drain()
            for sub in range(4):
                i0 = sub
                i1 = 4 + sub
                b0, o0 = i0 // 3, 129 * (i0 % 3)
                b1, o1 = i1 // 3, 129 * (i1 % 3)
                VE.tensor_scalar(out=at32[sub][:],
                                 in0=PS[b0][:, o0:o0 + 128],
                                 scalar1=rlbuf[:, sub:sub + 1],
                                 scalar2=None, op0=ALU.mult)
                VE.drain()
                VE.scalar_tensor_tensor(out=at32[sub][:],
                                        in0=PS[b1][:, o1:o1 + 128],
                                        scalar=rn1[:, sub:sub + 1],
                                        in1=at32[sub][:],
                                        op0=ALU.mult, op1=ALU.add)
                VE.bump(s_v, 1)
            AC.wait(s_v, s_v.n)
            for sub in range(4):
                AC.activation(out=SCR[:, 0:128], in_=at32[sub][:],
                              func=AF.Square,
                              accum_out=ssb[:, sub:sub + 1])
            AC.bump(s_a, 1)
            VE.wait(s_a, s_a.n)
            VE.tensor_scalar(out=ssb[:], in0=ssb[:], scalar1=1.0 / 128.0,
                             scalar2=1e-5, op0=ALU.mult, op1=ALU.add)
            VE.bump(s_v, 1)
            AC.wait(s_v, s_v.n)
            AC.activation(out=ssb[:], in_=ssb[:], func=AF.Ln)
            AC.drain()
            AC.activation(out=ssb[:], in_=ssb[:], func=AF.Exp, scale=-0.5)
            AC.bump(s_a, 1)
            VE.wait(s_a, s_a.n)
            for sub in range(4):
                qbg = 4 * Qb + sub
                VE.scalar_tensor_tensor(out=at32[sub][:], in0=at32[sub][:],
                                        scalar=ssb[:, sub:sub + 1],
                                        in1=sublnw_sb[:],
                                        op0=ALU.mult, op1=ALU.mult)
                VE.drain()
                VE.tensor_copy(out=abf[:], in_=at32[sub][:])
                VE.bump(s_v, 1)
                PE.wait(s_v, s_v.n)
                PE.transpose(PSb[3][:, 0:128], abf[:], ident[:])
                PE.bump(s_pe, 1)
                AC.wait(s_pe, s_pe.n)
                AC.copy(out=pstg[:, 128 * sub:128 * (sub + 1)],
                        in_=PSb[3][:, 0:128])
                AC.bump(s_a, 1)
                PE.wait(s_a, s_a.n)
                VE.wait(s_a, s_a.n)
            if Qb < 4:
                SP.wait(s_a, s_a.n)
                SP.dma_start(out=at_d.ap()[vh, :, 512 * Qb:512 * (Qb + 1)],
                             in_=pstg[:]).then_inc(d_at, 16)
                AC.wait(d_at, d_at.n)
            else:
                def mk_attn_out(vh_=vh, Qb_=Qb):
                    def go(g_):
                        rpar = PLregs["rpar"]
                        for par in range(2):
                            ctx = (g_.If_eq(rpar, 0) if par == 0
                                   else g_.Else())
                            with ctx:
                                g_.dma_start(
                                    out=sh_attn[par, vh_, :,
                                                512 * (Qb_ - 4):
                                                512 * (Qb_ - 3)],
                                    in_=pstg[:]).then_inc(d_ao.h, 16)
                    return go
                PL.wait(s_a, s_a.n)
                PL.wait(d_ao, d_ao.n)
                PL.raw(mk_attn_out())
                d_ao.n += 16
                AC.wait(d_ao, d_ao.n)
    PL.wait(d_ao, d_ao.n)
    SP.wait(d_at, d_at.n)
    ar.release()
    if os.environ.get("KTAP") == "at":
        PL.wait(d_at, d_at.n)
        PL.dma_start(out=y.rearrange("a b -> (a b)")[0:8 * 128 * ROWS]
                     .rearrange("(p w) -> p w", p=128),
                     in_=at_d.ap().rearrange("v p t -> (v p t)")
                     .rearrange("(p w) -> p w", p=128)).then_inc(d_pl, 16)
        PL.wait(d_pl, d_pl.n)
    ckpt("p4b")
    pair_barrier()
    phase_barrier()

    ar.mark()
    wo_sb = ar.alloc("wo_sb", [128, 16, D], BF16)
    lsc1_16 = ar.alloc("lsc1_16", [128, NT_OWN], F32)
    lsc2_16 = ar.alloc("lsc2_16", [128, NT_OWN], F32)
    oacc = [ar.alloc(f"oacc{i}", [128, D], F32) for i in range(2)]
    pat = [ar.alloc(f"pat{i}", [128, 8, 128], BF16) for i in range(2)]
    mat = [ar.alloc(f"mat{i}", [128, 8, 128], BF16) for i in range(2)]
    d_pa = [mksem("d_pa0"), mksem("d_pa1")]
    SP.dma_start(out=wo_sb[:],
                 in_=wo_c.ap().rearrange("(c p) d -> p c d", p=128)
                 ).then_inc(d_kv, 16)
    WO_READY = d_kv.n
    for tch in range(NT_OWN):
        tb = tch % 2
        SP.wait(s_v, s_v.n)
        SP.dma_start(out=oacc[tb][:], in_=xb_t[tch]).then_inc(d_xp[tb], 16)

        def mk_pat(dst_i=tb, tch_=tch):
            def go(g_):
                rpar = PLregs["rpar"]
                for par in range(2):
                    src_par = 1 - par
                    ctx = (g_.If_eq(rpar, 0) if par == 0 else g_.Else())
                    with ctx:
                        g_.dma_start(
                            out=pat[dst_i][:],
                            in_=sh_attn[src_par, :, :,
                                        128 * tch_:128 * (tch_ + 1)]
                            .rearrange("c p t -> p c t")
                            ).then_inc(d_pa[dst_i].h, 16)
            return go
        PL.wait(d_pa[tb], d_pa[tb].n)
        PL.wait(s_pe, s_pe.n)
        PL.raw(mk_pat())
        d_pa[tb].n += 16
        SP.wait(s_pe, s_pe.n)
        SP.dma_start(out=mat[tb][:],
                     in_=at_d.ap()[:, :, 128 * tch:128 * (tch + 1)]
                     .rearrange("c p t -> p c t")).then_inc(d_q[tb], 16)
        if tch == 0:
            PE.wait(d_kv, WO_READY)
        PE.wait(d_pa[tb], d_pa[tb].n)
        PE.wait(d_q[tb], d_q[tb].n)
        PE.wait(s_v, s_v.n)
        for fc in range(16):
            lt = (mat[tb][:, fc, :] if fc < 8
                  else pat[tb][:, fc - 8, :])
            for q4 in range(4):
                PE.matmul(PS[q4][:], lt, wo_sb[:, fc, 512 * q4:512 * (q4 + 1)],
                          start=(fc == 0), stop=(fc == 15))
        PE.bump(s_pe, 1)
        VE.wait(s_pe, s_pe.n)
        VE.wait(d_xp[tb], d_xp[tb].n)
        for q4 in range(4):
            VE.tensor_tensor(out=oacc[tb][:, 512 * q4:512 * (q4 + 1)],
                             in0=PS[q4][:],
                             in1=oacc[tb][:, 512 * q4:512 * (q4 + 1)],
                             op=ALU.add)
        VE.drain()
        VE.reduce_sum(stats16[:, tch, 0:1], oacc[tb][:], axis=AX.X)
        VE.drain()
        VE.tensor_scalar(out=stats16[:, tch, 0:1], in0=stats16[:, tch, 0:1],
                         scalar1=1.0 / D, scalar2=None, op0=ALU.mult)
        VE.drain()
        VE.tensor_scalar(out=SCR[:], in0=oacc[tb][:],
                         scalar1=stats16[:, tch, 0:1],
                         scalar2=None, op0=ALU.subtract)
        VE.drain()
        VE.tensor_tensor(out=SCR[:], in0=SCR[:], in1=SCR[:], op=ALU.mult)
        VE.drain()
        VE.reduce_sum(stats16[:, tch, 1:2], SCR[:], axis=AX.X)
        VE.bump(s_v, 1)
        SP.wait(s_v, s_v.n)
        SP.dma_start(out=xb_t[tch], in_=oacc[tb][:]).then_inc(d_st, 16)
        VE.wait(d_st, d_st.n)
    VE.tensor_scalar(out=stats16[:, :, 1], in0=stats16[:, :, 1],
                     scalar1=1.0 / D, scalar2=1e-5,
                     op0=ALU.mult, op1=ALU.add)
    VE.drain()
    emit_rsqrt(VE, stats16[:, :, 1], stats16[:, :, 1], lsc1_16[:], lsc2_16[:])
    VE.bump(s_v, 1)
    ar.release()
    ar.release()
    ckpt("p4")

    y_t = y.rearrange("(n p) d -> n p d", p=128)
    emit_ffn("f2", y_t, False)
    ckpt("p5")

    for r in ALL:
        r.wait(d_st, d_st.n)
        r.wait(d_out, d_out.n)

    kt = os.environ.get("KTRUNC", "")
    if kt:
        snap = CKPT[kt]
        for r in ALL:
            del r.ops[snap[id(r)]:]
    with nc.Block() as block:
        block.sync(lambda e: SP.replay(e))
        block.gpsimd(lambda e: PL.replay(e))
        block.tensor(lambda e: PE.replay(e))
        block.vector(lambda e: VE.replay(e))
        block.scalar(lambda e: AC.replay(e))

    for g in sem_guards:
        g.__exit__(None, None, None)
    return nc


def _rope_tables(positions):
    inv = 1.0 / 10000.0 ** (np.arange(0, 64, 2)[:32].astype(np.float64) / 64)
    ang = np.outer(positions, inv)
    return np.cos(ang).astype(np.float32), np.sin(ang).astype(np.float32)


def make_in_maps(ii):
    x_full = ii["L0_attn_a_hidden_states"].reshape(NTOK, D)
    wkvmbT = np.ascontiguousarray(np.concatenate(
        [ii["L0_attn_a_Wk"].T, ii["L0_attn_a_Wv"].T,
         ii["L0_attn_a_Wmb"].T], axis=1))
    wqgT = np.ascontiguousarray(np.concatenate(
        [ii["L0_attn_a_Wq"].T, ii["L0_attn_a_Wg"].T], axis=1))
    bias_kvmb = np.concatenate(
        [ii["L0_attn_a_bk"], ii["L0_attn_a_bv"],
         ii["L0_attn_a_bmb"]])[None, :].astype(np.float32)
    bias_qg = np.concatenate(
        [ii["L0_attn_a_bq"], ii["L0_attn_a_bg"]])[None, :].astype(np.float32)
    mem = np.ascontiguousarray(ii["L0_attn_a_memory"].reshape(H * FD, 128))
    normT = np.ascontiguousarray(ii["L0_attn_a_norm"].T)
    def roll_cols(w, par):
        return w if par == 0 else np.concatenate(
            [w[:, 1024:], w[:, :1024]], axis=1)
    wq_b, wk_b, wv_b = (ii["L0_attn_b_wq"].T, ii["L0_attn_b_wk"].T,
                        ii["L0_attn_b_wv"].T)
    wo_b = ii["L0_attn_b_wout"].T
    wqkvbT_p, woT_p = [], []
    for par in range(2):
        wqkvbT_p.append(np.ascontiguousarray(np.concatenate(
            [roll_cols(wq_b, par), roll_cols(wk_b, par),
             roll_cols(wv_b, par)], axis=1)))
        woT_p.append(np.ascontiguousarray(
            wo_b if par == 0 else np.concatenate(
                [wo_b[1024:], wo_b[:1024]], axis=0)))
    lam = np.tile(np.concatenate(
        [ii["L0_attn_b_lambda_q1"], ii["L0_attn_b_lambda_q2"],
         ii["L0_attn_b_lambda_k1"], ii["L0_attn_b_lambda_k2"]]
    )[None, :].astype(np.float32), (128, 1))
    sublnw_r = np.ascontiguousarray(
        np.tile(ii["L0_attn_b_subln_weight"][None, :], (128, 1)))
    ident = np.eye(128, dtype=ml_dtypes.bfloat16)
    onesrow = np.ones((1, 128), dtype=ml_dtypes.bfloat16)

    in_maps = []
    for c in range(NCORES):
        par = c % 2
        pos = np.arange(2048 * par, 2048 * par + ROWS)
        cq, sq = _rope_tables(pos)
        in_maps.append({
            "xo": np.ascontiguousarray(x_full[ROWS * c:ROWS * (c + 1)]),
            "xk": np.ascontiguousarray(x_full[KROWS * par:KROWS * (par + 1)]),
            "wkvmbT": wkvmbT, "wqgT": wqgT,
            "bias_kvmb": bias_kvmb, "bias_qg": bias_qg,
            "mem": mem, "normT": normT,
            "wqkvbT": wqkvbT_p[par], "woT": woT_p[par], "lam": lam,
            "sublnw_r": sublnw_r,
            "cosq": np.ascontiguousarray(cq * QSCALE),
            "sinq": np.ascontiguousarray(sq * QSCALE),
            "cosk": cq, "sink": sq,
            "wg_f": np.ascontiguousarray(ii["L0_ffn_w_gate"]),
            "wu_f": np.ascontiguousarray(ii["L0_ffn_w_up"]),
            "wd_f": np.ascontiguousarray(ii["L0_ffn_w_down"]),
            "ident": ident, "onesrow": onesrow,
            "meta": np.array([[par, c, 0, 0, 0, 0, 0, 0]], np.int32),
        })

    return in_maps


def kernel(**inputs):
    ii = {k: np.asarray(v) for k, v in inputs.items()}
    if "nc" not in _CACHE:
        _CACHE["nc"] = build_program()
    nc = _CACHE["nc"]
    in_maps = make_in_maps(ii)
    res = run_bass_kernel_spmd(nc, in_maps, core_ids=list(range(NCORES)))
    out = np.concatenate([res.results[c]["y"] for c in range(NCORES)], axis=0)
    if DEBUG_TAPS:
        kernel.taps = {t: np.concatenate(
            [res.results[c]["tap_" + t] for c in range(NCORES)], axis=0)
            for t in DEBUG_TAPS}
    return out.reshape(4, SEQ, D)
